# version 22
# baseline (speedup 1.0000x reference)
import numpy as np
import ml_dtypes
import concourse.bass as bass
import concourse.mybir as mybir
from concourse.bass_utils import run_bass_kernel_spmd

F32 = mybir.dt.float32
BF16 = mybir.dt.bfloat16
I32 = mybir.dt.int32
AF = mybir.ActivationFunctionType
ALU = mybir.AluOpType
AX = mybir.AxisListType

B, T, D, H, HD, WIN, FF = 32, 2048, 1024, 16, 64, 128, 3072
NCORES = 8
SEQ_PER_CORE = B // NCORES
NT = 512
TILES_PER_SEQ = T // NT
NTILES = SEQ_PER_CORE * TILES_PER_SEQ
TCN = NT // 128
DCN = D // 128
HCN = FF // 128
EPS_LN = 1e-5
EPS_L2 = 1e-6
C127 = float(np.float32(1.0 / 127.0))
C127SQ = float(np.float32(1.0 / 16129.0))


class _Evt:
    __slots__ = ("sem", "ord", "used", "val")

    def __init__(self, sem, ordn):
        self.sem, self.ord, self.used, self.val = sem, ordn, False, None


class _Buf:
    __slots__ = ("name", "ready", "users")

    def __init__(self, name):
        self.name, self.ready, self.users = name, None, []


_ENG_SEM = {"pe": "pe", "act": "act", "dve": "dve", "sp": "sp", "gp": "gp"}
_DMA_ENGS = ("sp", "gp")
NDMASEM = 6


class _Plan:
    def __init__(self):
        self.ops = {e: [] for e in _ENG_SEM}
        self.nsem = {}
        self.marks = {e: {} for e in _ENG_SEM}
        self.dmaidx = {e: 0 for e in _DMA_ENGS}
        self.last_on_sem = {}

    def op(self, eng, fn, reads=(), writes=(), tag="", ndma=1):
        waits = {}

        def need(evt):
            if evt is None:
                return
            m = self.marks[eng].get(evt.sem)
            if m is not None and m >= evt.ord:
                return
            cur = waits.get(evt.sem)
            if cur is None or evt.ord > cur.ord:
                waits[evt.sem] = evt

        for b in reads:
            need(b.ready)
        for b in writes:
            need(b.ready)
            for u in b.users:
                need(u)
        if eng in _DMA_ENGS and ndma > 0:
            slot = self.dmaidx[eng] % NDMASEM
            self.dmaidx[eng] += 1
            sem = f"{eng}{slot}"
            need(self.last_on_sem.get(sem))
        elif eng == "gp":
            sem = "gpc"
        else:
            sem = _ENG_SEM[eng]
        for s_, evt in waits.items():
            evt.used = True
            self.marks[eng][s_] = evt.ord
        ordn = self.nsem.get(sem, 0)
        self.nsem[sem] = ordn + 1
        evt = _Evt(sem, ordn)
        if eng in _DMA_ENGS and ndma > 0:
            evt.used = True
            self.last_on_sem[sem] = evt
        self.ops[eng].append((list(waits.values()), fn, evt, tag,
                              ndma if eng in _DMA_ENGS else 0))
        for b in writes:
            b.ready = evt
            b.users = []
        for b in reads:
            b.users.append(evt)
        return evt

    def finalize(self):
        cnt = {}
        for e in _ENG_SEM:
            for (_w, _f, evt, _t, nd) in self.ops[e]:
                if evt.used:
                    inc = 16 * nd if nd > 0 else 1
                    c = cnt.get(evt.sem, 0) + inc
                    cnt[evt.sem] = c
                    evt.val = c
        return cnt


def _make_consts():
    slopes = (2.0 ** (-(8.0 * (np.arange(H) / 15.0)))).astype(np.float64)
    k = np.arange(128)[:, None]
    q = np.arange(128)[None, :]
    eb = np.zeros((2, 128, H * 128), np.float64)
    rel0 = k - q
    rel1 = 128 + k - q
    for h in range(H):
        e0 = np.where(rel0 >= 0, np.exp(-np.abs(rel0) * slopes[h]), 0.0)
        e1 = np.where(rel1 <= 127, np.exp(-np.abs(rel1) * slopes[h]), 0.0)
        eb[0][:, 128 * h:128 * h + 128] = e0
        eb[1][:, 128 * h:128 * h + 128] = e1
    return eb.astype(ml_dtypes.bfloat16), np.eye(128, dtype=ml_dtypes.bfloat16)


def build_nc(n_iters=NTILES, dbg=False):
    nc = bass.Bass("TRN2", target_bir_lowering=False)

    x_in = nc.dram_tensor("x", [SEQ_PER_CORE, T, D], F32, kind="ExternalInput").ap()
    w_in = {}
    for nm, shp, dt in [("wq", [D, D], F32), ("wk", [D, D], F32),
                        ("wv", [D, D], F32), ("wo", [D, D], F32),
                        ("wg", [FF, D], I32), ("wu", [FF, D], I32),
                        ("wd", [D, FF], I32)]:
        w_in[nm] = nc.dram_tensor(nm, shp, dt, kind="ExternalInput").ap()
    eb_in = nc.dram_tensor("eb", [2, 128, H * 128], BF16, kind="ExternalInput").ap()
    id_in = nc.dram_tensor("ident", [128, 128], BF16, kind="ExternalInput").ap()
    out = nc.dram_tensor("out", [SEQ_PER_CORE, T, D], F32, kind="ExternalOutput").ap()

    wT = {nm: nc.dram_tensor(nm + "T_s", [2, 128, 4096], BF16, kind="Internal").ap()
          for nm in ("wq", "wk", "wv", "wo")}
    wT["wg"] = nc.dram_tensor("wgT_s", [24, 128, 1024], BF16, kind="Internal").ap()
    wT["wu"] = nc.dram_tensor("wuT_s", [24, 128, 1024], BF16, kind="Internal").ap()
    wT["wd"] = nc.dram_tensor("wdT_s", [24, 128, 1024], BF16, kind="Internal").ap()

    dbg_outs = {}
    if dbg:
        for nm, shp, dt in [
            ("d_xn", [128, 4096], BF16), ("d_kf", [128, 4096], BF16),
            ("d_vp", [128, 4160], BF16), ("d_qf", [128, 4096], BF16),
            ("d_pt", [128, 4096], BF16), ("d_sb", [128, 4096], BF16),
            ("d_hid", [128, 12288], BF16),
        ]:
            dbg_outs[nm] = nc.dram_tensor(nm, shp, dt, kind="ExternalOutput").ap()

    P = _Plan()
    import contextlib
    stack = contextlib.ExitStack()
    sb = lambda name, shape, dt: stack.enter_context(nc.sbuf_tensor(name, shape, dt))
    ps = lambda name, shape, dt: stack.enter_context(nc.psum_tensor(name, shape, dt))
    mksem = lambda name: stack.enter_context(nc.semaphore(name))

    with stack:
        XL = sb("XL", [128, 4096], F32)
        XE = sb("XE", [128, 4096], F32)
        XN = [sb(f"XN{i}", [128, 4096], BF16) for i in range(2)]
        TM = sb("TM", [128, 4096], BF16)
        QF = sb("QF", [128, 4096], BF16)
        KF = [sb(f"KF{i}", [128, 4096], BF16) for i in range(2)]
        VP = [sb(f"VP{i}", [128, 4160], BF16) for i in range(2)]
        PT = sb("PT", [128, 4096], BF16)
        SBUF_S = sb("SBS", [128, 4096], BF16)
        HID = sb("HID", [128, 12288], BF16)
        GBUF = sb("GBUF", [128, 1024], F32)
        AOF = sb("AOF", [128, 1024], BF16)
        SQ = sb("SQ", [128, 1024], F32)
        TSC = sb("TSC", [128, 1024], BF16)
        EB = sb("EB", [128, 4096], BF16)
        IDT = sb("IDT", [128, 128], BF16)
        CST = sb("CST", [128, 4], F32)
        STAT = sb("STAT", [128, 256], F32)
        WA = [sb(f"WA{i}", [128, 4096], BF16) for i in range(3)]
        WGU = [sb(f"WGU{i}", [128, 3072], BF16) for i in range(3)]
        WD = [sb(f"WD{i}", [128, 2048], BF16) for i in range(2)]

        psA = ps("psA", [128, 2048], F32)
        psB = ps("psB", [128, 2048], F32)

        sems = {n: mksem(n) for n in ("pe", "act", "dve", "gpc")}
        for e in _DMA_ENGS:
            for i in range(NDMASEM):
                sems[f"{e}{i}"] = mksem(f"{e}{i}")

        bXL, bQF, bAOF = (_Buf(n) for n in ("XL", "QF", "AOF"))
        bXE4 = [_Buf(f"XE{t}") for t in range(TCN)]
        bTM4 = [_Buf(f"TM{i}") for i in range(TCN)]
        bXN = [[_Buf(f"XN{i}_{t}") for t in range(TCN)] for i in range(2)]
        bKF = [_Buf(f"KF{i}") for i in range(2)]
        bVP = [_Buf(f"VP{i}") for i in range(2)]
        bPT = [[_Buf(f"PT{c}_{b}") for b in range(4)] for c in range(2)]
        bSB4 = [_Buf(f"SB{q}") for q in range(TCN)]
        bHID12 = [_Buf(f"HID{i}") for i in range(12)]
        bG = _Buf("G")
        bSQ2 = [_Buf(f"SQ{i}") for i in range(2)]
        bTSC = [_Buf(f"TSC{i}") for i in range(2)]
        bEB, bIDT, bCST = _Buf("EB"), _Buf("IDT"), _Buf("CST")
        bNRM = [_Buf(f"NRM{i}") for i in range(8)]
        bJNK = _Buf("JNK")
        bQF4 = [_Buf(f"QFs{i}") for i in range(4)]
        bPT2 = [_Buf(f"PTs{i}") for i in range(2)]
        bRZ = _Buf("RZ")
        bpsA = [_Buf(f"psA{b}") for b in range(4)]
        bpsB = [_Buf(f"psB{b}") for b in range(4)]
        bWT = {n: _Buf("wt_" + n) for n in ("wq", "wk", "wv", "wo", "wg", "wu", "wd")}
        bWA = [_Buf(f"WA{i}") for i in range(3)]
        bWGU = [_Buf(f"WGU{i}") for i in range(3)]
        bWD = [_Buf(f"WD{i}") for i in range(2)]

        LNS4, LNSS4, LNM4, LNMSQ4, LNV4, LNR4, LNMR4 = (4 * i for i in range(7))
        RZ0 = 160
        st = lambda c, w=1: STAT[:, c:c + w]
        bLN1 = _Buf("LN1")

        wa_i, gu_i, wd_i = [0], [0], [0]

        def load_wa(nm, eh, tag):
            s = wa_i[0] % 3
            wa_i[0] += 1
            ap, bf = WA[s], bWA[s]
            src_ap = wT[nm][eh]
            P.op("sp", (lambda e, sem, ap=ap, src_ap=src_ap:
                        e.dma_start(ap[:, :], src_ap).then_inc(sem, 16)),
                 reads=[bWT[nm]], writes=[bf], tag=tag)
            return ap, bf

        gu9_i = [0]
        bGU9 = [_Buf(f"GU9_{i}") for i in range(9)]

        def load_gu_hc(nm, hc, tag):
            s = gu9_i[0] % 9
            gu9_i[0] += 1
            ap, off = WGU[s // 3], 1024 * (s % 3)
            bf = bGU9[s]
            P.op("sp", (lambda e, sem, ap=ap, off=off, src=wT[nm][hc]:
                        e.dma_start(ap[:, off:off + 1024], src).then_inc(sem, 16)),
                 reads=[bWT[nm]], writes=[bf, bWGU[s // 3]], tag=tag)
            return (ap, off), bf

        def load_wd2(hc2, tag):
            s = wd_i[0] % 2
            wd_i[0] += 1
            ap, bf = WD[s], bWD[s]
            P.op("sp", (lambda e, sem, ap=ap, hc2=hc2: e.dma_start(
                ap[:, :].rearrange("p (i q) -> p i q", i=2),
                wT["wd"][2 * hc2:2 * hc2 + 2].rearrange("i p q -> p i q")
                ).then_inc(sem, 16)),
                 reads=[bWT["wd"]], writes=[bf], tag=tag)
            return ap, bf

        pin_pool = [(WD[0][:, 0:1024], bWD[0]), (WD[1][:, 0:1024], bWD[1])] + [
            (QF[:, 1024 * i:1024 * i + 1024], bQF4[i]) for i in range(4)]
        pinw_pool = [(WGU[i][:, 0:1536], bWGU[i]) for i in range(3)] + [
            (PT[:, 2048 * i:2048 * i + 1536], bPT2[i]) for i in range(2)]

        def _prep_in(nm, rows0, ncols, pool, idx, cols0=0):
            s = idx[0] % len(pool)
            q = "gp"
            idx[0] += 1
            ap, bf = pool[s]
            P.op(q, (lambda e, sem, ap=ap, nm=nm, rows0=rows0, ncols=ncols,
                     cols0=cols0:
                     e.dma_start(ap[:, 0:ncols],
                                 w_in[nm][rows0:rows0 + 128,
                                          cols0:cols0 + ncols]
                                 ).then_inc(sem, 16)),
                 writes=[bf], tag=f"pin_{nm}_{rows0}_{cols0}")
            return ap, bf

        def _prep_T(inap, inbf, nblk, bank, tag, ps=None, bps=None):
            ps = psA if ps is None else ps
            bps = bpsA if bps is None else bps
            def fn(e, inap=inap, nblk=nblk, bank=bank, ps=ps):
                last = None
                bc = ps[:, 512 * bank:512 * bank + 512].bitcast(BF16)
                if nblk > 8:
                    bc = ps[:, 512 * bank:512 * bank + 1024].bitcast(BF16)
                for i in range(nblk):
                    last = e.transpose(bc[:, 128 * i:128 * i + 128],
                                       inap[:, 128 * i:128 * i + 128], IDT[:, :])
                return last
            wrts = [bps[bank]] + ([bps[bank + 1]] if nblk > 8 else [])
            P.op("pe", fn, reads=[inbf, bIDT], writes=wrts, tag=tag)
            return wrts

        def prep_attn(nm):
            for eh in range(2):
                s = wa_i[0] % 3
                wa_i[0] += 1
                stg, bstg = WA[s], bWA[s]
                for r in range(4):
                    inap, inbf = _prep_in(nm, 512 * eh + 128 * r, 1024,
                                          pin_pool, wd_i)
                    pb = _prep_T(inap, inbf, 8, r % 4, f"pT_{nm}{eh}_{r}")
                    P.op("act", (lambda e, stg=stg, r=r, bank=r % 4: e.copy(
                        stg[:, :].rearrange("p (dc q) -> p dc q", q=512)[:, :, 128 * r:128 * r + 128],
                        psA[:, 512 * bank:512 * bank + 512].bitcast(BF16).rearrange(
                            "p (dc q) -> p dc q", dc=8))),
                        reads=pb, writes=[bstg], tag=f"pEv_{nm}{eh}_{r}")
                P.op("sp", (lambda e, sem, stg=stg, nm=nm, eh=eh:
                            e.dma_start(wT[nm][eh], stg[:, :]).then_inc(sem, 16)),
                     reads=[bstg], writes=[bWT[nm]], tag=f"pS_{nm}{eh}")

        def prep_gu_chunk(nm, q8, pool, ps=None, bps=None):
            pps = psA if ps is None else ps
            s = gu_i[0] % 3
            gu_i[0] += 1
            stg, bstg = WGU[s], bWGU[s]
            for j in range(3):
                inap, inbf = _prep_in(nm, 384 * q8 + 128 * j, 1024,
                                      pool, wd_i)
                pb = _prep_T(inap, inbf, 8, j % 4, f"pT_{nm}{q8}_{j}",
                             ps=ps, bps=bps)
                P.op("act", (lambda e, stg=stg, j=j, bank=j % 4, pps=pps: e.copy(
                    stg[:, :].rearrange("p (dc q) -> p dc q", q=384)[:, :, 128 * j:128 * j + 128],
                    pps[:, 512 * bank:512 * bank + 512].bitcast(BF16).rearrange(
                        "p (dc q) -> p dc q", dc=8))),
                    reads=pb, writes=[bstg], tag=f"pEv_{nm}{q8}_{j}")
            for j in range(3):
                P.op("sp", (lambda e, sem, stg=stg, nm=nm, q8=q8, j=j:
                            e.dma_start(
                    wT[nm][3 * q8 + j].rearrange("p (dc e) -> p dc e", dc=8),
                    stg[:, :].rearrange("p (dc j e) -> p j dc e",
                                        j=3, e=128)[:, j]).then_inc(sem, 16)),
                     reads=[bstg], writes=[bWT[nm]], tag=f"pS_{nm}{q8}_{j}")

        def prep_wd():
            for half in range(2):
                for r in range(8):
                    inap, inbf = _prep_in("wd", 128 * r, 1536, pinw_pool, gu_i,
                                          cols0=1536 * half)
                    pb = _prep_T(inap, inbf, 12, 2 * (r % 2), f"pT_wd{half}_{r}")
                    P.op("act", (lambda e, r=r, bank=2 * (r % 2): e.copy(
                        HID[:, :].rearrange("p (hc q) -> p hc q", q=1024)[:, 0:12, 128 * r:128 * r + 128],
                        psA[:, 512 * bank:512 * bank + 1024].bitcast(BF16)[:, 0:1536].rearrange(
                            "p (hc q) -> p hc q", hc=12))),
                        reads=pb, writes=bHID12, tag=f"pEv_wd{half}_{r}")
                for i in range(12):
                    P.op("sp", (lambda e, sem, i=i, half=half:
                                e.dma_start(wT["wd"][12 * half + i],
                                            HID[:, 1024 * i:1024 * i + 1024]
                                            ).then_inc(sem, 16)),
                         reads=bHID12, writes=[bWT["wd"]], tag=f"pS_wd{half}_{i}")

        def load_x(g, tag):
            s1, j1 = divmod(g, TILES_PER_SEQ)
            src = x_in[s1, NT * j1:NT * (j1 + 1), :].rearrange(
                "(tc tp) d -> tp tc d", tp=128)
            P.op("gp", (lambda e, sem, src=src: e.dma_start(
                XL[:, :].rearrange("p (tc d) -> p tc d", tc=TCN),
                src).then_inc(sem, 16)), writes=[bXL], tag=tag)

        def load_xe(g, tag):
            s1, j1 = divmod(g, TILES_PER_SEQ)
            src = x_in[s1, NT * j1:NT * (j1 + 1), :].rearrange(
                "(tc tp) d -> tp tc d", tp=128)
            P.op("gp", (lambda e, sem, src=src: e.dma_start(
                XE[:, :].rearrange("p (tc d) -> p tc d", tc=TCN),
                src).then_inc(sem, 16)), writes=bXE4, tag=tag)

        def ln_stats(gt):
            for tc in range(TCN):
                xsl = XL[:, 1024 * tc:1024 * (tc + 1)]
                P.op("act", (lambda e, xsl=xsl, tc=tc: e.activation(
                    GBUF[:, :], xsl, AF.Copy,
                    accum_out=st(LNS4 + tc))),
                    reads=[bXL], writes=[bG, bLN1], tag=f"lnsum{gt}_{tc}")
                P.op("act", (lambda e, xsl=xsl, tc=tc: e.activation(
                    GBUF[:, :], xsl, AF.Square,
                    accum_out=st(LNSS4 + tc))),
                    reads=[bXL], writes=[bG, bLN1], tag=f"lnss{gt}_{tc}")
            P.op("gp", lambda e: e.tensor_scalar_mul(
                st(LNM4, 4), st(LNS4, 4), -1.0 / D),
                reads=[bLN1], writes=[bLN1], ndma=0)
            P.op("gp", lambda e: e.tensor_mul(
                st(LNMSQ4, 4), st(LNM4, 4), st(LNM4, 4)),
                reads=[bLN1], writes=[bLN1], ndma=0)
            P.op("gp", lambda e: e.tensor_scalar_mul(
                st(LNV4, 4), st(LNSS4, 4), 1.0 / D),
                reads=[bLN1], writes=[bLN1], ndma=0)
            P.op("gp", lambda e: e.tensor_sub(
                st(LNV4, 4), st(LNV4, 4), st(LNMSQ4, 4)),
                reads=[bLN1], writes=[bLN1], ndma=0)
            P.op("gp", lambda e: e.tensor_scalar_add(
                st(LNV4, 4), st(LNV4, 4), EPS_LN),
                reads=[bLN1], writes=[bLN1], ndma=0)
            P.op("act", lambda e: e.activation(
                st(LNR4, 4), st(LNV4, 4), AF.Sqrt),
                reads=[bLN1], writes=[bLN1])
            P.op("dve", lambda e: e.reciprocal(st(LNR4, 4), st(LNR4, 4)),
                 reads=[bLN1], writes=[bLN1])
            P.op("gp", lambda e: e.tensor_mul(
                st(LNMR4, 4), st(LNM4, 4), st(LNR4, 4)),
                reads=[bLN1], writes=[bLN1], ndma=0)

        def ln_apply(gt):
            for tc in range(TCN):
                P.op("act", (lambda e, tc=tc: e.activation(
                    TM[:, 1024 * tc:1024 * (tc + 1)],
                    XL[:, 1024 * tc:1024 * (tc + 1)], AF.Copy,
                    scale=st(LNR4 + tc))),
                    reads=[bXL, bLN1], writes=[bTM4[tc]], tag=f"lnap{gt}_{tc}")
                P.op("dve", (lambda e, tc=tc: e.tensor_tensor(
                    TM[:, 1024 * tc:1024 * (tc + 1)],
                    TM[:, 1024 * tc:1024 * (tc + 1)],
                    st(LNMR4 + tc).broadcast_to([128, 1024]), ALU.add)),
                    reads=[bLN1, bTM4[tc]], writes=[bTM4[tc]],
                    tag=f"lnad{gt}_{tc}")

        def xn_T(gt, tcs):
            p1 = gt % 2
            for tc in tcs:
                def tfn(e, tc=tc):
                    last = None
                    bc = psA[:, 512 * tc:512 * tc + 512].bitcast(BF16)
                    for dc in range(DCN):
                        last = e.transpose(
                            bc[:, 128 * dc:128 * dc + 128],
                            TM[:, 1024 * tc + 128 * dc:1024 * tc + 128 * dc + 128],
                            IDT[:, :])
                    return last
                P.op("pe", tfn, reads=[bTM4[tc], bIDT], writes=[bpsA[tc]],
                     tag=f"xnT{gt}_{tc}")
                P.op("act", (lambda e, tc=tc, p1=p1: e.copy(
                    XN[p1][:, :].rearrange("p (dc t) -> p dc t", dc=DCN)[:, :, 128 * tc:128 * tc + 128],
                    psA[:, 512 * tc:512 * tc + 512].bitcast(BF16).rearrange(
                        "p (dc t) -> p dc t", dc=DCN))),
                    reads=[bpsA[tc]], writes=[bXN[p1][tc]], tag=f"xnEv{gt}_{tc}")

        def norm_apply(g1, which, eh, tc):
            r = 2 * tc + eh
            SS8, N8 = 32 + 16 * r, 40 + 16 * r
            bN = bNRM[r]
            sqo = 512 * (tc % 2)
            bS = bSQ2[tc % 2]
            P.op("act", (lambda e, tc=tc, sqo=sqo: e.activation(
                SQ[:, sqo:sqo + 512], psA[:, 512 * tc:512 * tc + 512], AF.Square)),
                reads=[bpsA[tc]], writes=[bS], tag=f"{which}sq{g1}_{eh}_{tc}")
            P.op("dve", (lambda e, SS8=SS8, sqo=sqo: e.tensor_reduce(
                st(SS8, 8).rearrange("p (h u) -> p h u", u=1),
                SQ[:, sqo:sqo + 512].rearrange("p (h c) -> p h c", h=8),
                AX.X, ALU.add)), reads=[bS], writes=[bN])
            P.op("act", (lambda e, SS8=SS8, N8=N8: e.activation(
                st(N8, 8), st(SS8, 8), AF.Sqrt)), reads=[bN], writes=[bN])
            P.op("dve", (lambda e, N8=N8: e.tensor_scalar_max(
                st(N8, 8), st(N8, 8), EPS_L2)), reads=[bN], writes=[bN])
            P.op("dve", (lambda e, N8=N8: e.reciprocal(st(N8, 8), st(N8, 8))),
                 reads=[bN], writes=[bN])
            P.op("dve", (lambda e, eh=eh, tc=tc, N8=N8: e.tensor_tensor(
                TM[:, 1024 * tc + 512 * eh:1024 * tc + 512 * eh + 512].rearrange(
                    "p (h c) -> p h c", h=8),
                psA[:, 512 * tc:512 * tc + 512].rearrange("p (h c) -> p h c", h=8),
                st(N8, 8).unsqueeze(2).broadcast_to([128, 8, 64]), ALU.mult)),
                reads=[bpsA[tc], bN], writes=[bTM4[tc]], tag=f"{which}nrm{g1}_{eh}_{tc}")

        def kq_transpose(g1, DST, bDST, tag, tcs):
            for tc in tcs:
                def tfn(e, tc=tc):
                    last = None
                    bc = psA[:, 512 * tc:512 * tc + 512].bitcast(BF16)
                    for ec in range(DCN):
                        last = e.transpose(
                            bc[:, 128 * ec:128 * ec + 128],
                            TM[:, 1024 * tc + 128 * ec:1024 * tc + 128 * ec + 128],
                            IDT[:, :])
                    return last
                P.op("pe", tfn, reads=[bTM4[tc], bIDT], writes=[bpsA[tc]],
                     tag=f"{tag}{g1}_{tc}")
                P.op("act", (lambda e, tc=tc, DST=DST: e.copy(
                    DST[:, :].rearrange("p (ec t) -> p ec t", ec=DCN)[:, :, 128 * tc:128 * tc + 128],
                    psA[:, 512 * tc:512 * tc + 512].bitcast(BF16).rearrange(
                        "p (ec t) -> p ec t", ec=DCN))),
                    reads=[bpsA[tc]], writes=[bDST], tag=f"{tag}Ev{g1}_{tc}")

        def proj_eh(g1, eh, wslot, p1, tag):
            wap, wab = wslot
            for tc in range(TCN):
                def fn(e, eh=eh, tc=tc, wap=wap, p1=p1):
                    last = None
                    for dc in range(DCN):
                        last = e.matmul(
                            psA[:, 512 * tc:512 * tc + 512],
                            XN[p1][:, 512 * dc + 128 * tc:512 * dc + 128 * tc + 128],
                            wap[:, 512 * dc:512 * dc + 512],
                            start=(dc == 0), stop=(dc == DCN - 1))
                    return last
                P.op("pe", fn, reads=[bXN[p1][tc], wab], writes=[bpsA[tc]],
                     tag=f"{tag}{g1}_{eh}_{tc}")
                yield tc

        def ffn_unit(g, hc, pf):
            p = g % 2
            (gap, goff), gab = pf.pop(0)
            (uap, uoff), uab = pf.pop(0)
            if hc + 2 < HCN:
                pf.append(load_gu_hc("wg", hc + 2, f"wg{g}_{hc + 2}"))
                pf.append(load_gu_hc("wu", hc + 2, f"wu{g}_{hc + 2}"))
            go = 512 * (hc % 2)
            uo = 1024 + 512 * (hc % 2)

            def gufn(e, gap=gap, goff=goff, uap=uap, uoff=uoff, p=p,
                     go=go, uo=uo):
                last = None
                for ap, off, po in ((gap, goff, go), (uap, uoff, uo)):
                    for dc in range(DCN):
                        last = e.matmul(
                            psB[:, po:po + 512],
                            ap[:, off + 128 * dc:off + 128 * dc + 128],
                            XN[p][:, 512 * dc:512 * dc + 512],
                            start=(dc == 0), stop=(dc == DCN - 1))
                return last
            P.op("pe", gufn, reads=bXN[p] + [gab, uab],
                 writes=[bpsB[hc % 2], bpsB[2 + hc % 2]], tag=f"gate{g}_{hc}")
            P.op("act", (lambda e, hc=hc, go=go: e.activation(
                TSC[:, 512 * (hc % 2):512 * (hc % 2) + 512], psB[:, go:go + 512],
                AF.Silu, scale=CST[:, 1:2])),
                reads=[bpsB[hc % 2], bCST], writes=[bTSC[hc % 2]],
                tag=f"silu{g}_{hc}")
            P.op("dve", (lambda e, hc=hc, uo=uo: e.tensor_tensor(
                HID[:, 512 * hc:512 * hc + 512], psB[:, uo:uo + 512],
                TSC[:, 512 * (hc % 2):512 * (hc % 2) + 512], ALU.mult)),
                reads=[bpsB[2 + hc % 2], bTSC[hc % 2]], writes=[bHID12[hc // 2]],
                tag=f"hid{g}_{hc}")

        _DFB = {0: (psA, 0), 1: (psA, 1024), 2: (psB, 0), 3: (psB, 1024)}

        def down_group(g, hc2):
            wdap, wdb_ = load_wd2(hc2, f"wd{g}_{hc2}")

            def dfn(e, hc2=hc2, wdap=wdap):
                last = None
                for i in range(2):
                    hc = 2 * hc2 + i
                    for tc in range(TCN):
                        pt, off = _DFB[tc]
                        for dh in range(2):
                            last = e.matmul(
                                pt[:, off + 512 * dh:off + 512 * dh + 512],
                                HID[:, 512 * hc + 128 * tc:512 * hc + 128 * tc + 128],
                                wdap[:, 1024 * i + 512 * dh:1024 * i + 512 * dh + 512],
                                start=(hc == 0), stop=(hc == HCN - 1))
                return last
            wr = [bpsA[0], bpsA[1], bpsA[2], bpsA[3],
                  bpsB[0], bpsB[1], bpsB[2], bpsB[3]]
            P.op("pe", dfn, reads=[bHID12[hc2], wdb_], writes=wr,
                 tag=f"down{g}_{hc2}")

        def epilogue_tc(g, tc):
            s1, j1 = divmod(g, TILES_PER_SEQ)
            pt, off = _DFB[tc]
            bp = bpsA if tc < 2 else bpsB
            P.op("act", (lambda e, pt=pt, off=off: e.activation(
                GBUF[:, :], pt[:, off:off + 1024], AF.Silu, scale=CST[:, 2:3])),
                reads=[bp[2 * (tc % 2)], bp[2 * (tc % 2) + 1], bCST],
                writes=[bG], tag=f"gsilu{g}_{tc}")
            P.op("dve", (lambda e, tc=tc: e.tensor_tensor(
                GBUF[:, :], GBUF[:, :],
                SBUF_S[:, 1024 * tc:1024 * tc + 1024], ALU.mult)),
                reads=[bG, bSB4[tc]], writes=[bG], tag=f"sg{g}_{tc}")
            P.op("dve", (lambda e, tc=tc: e.tensor_tensor(
                XE[:, 1024 * tc:1024 * tc + 1024],
                GBUF[:, :],
                XE[:, 1024 * tc:1024 * tc + 1024], ALU.add)),
                reads=[bG, bXE4[tc]], writes=[bXE4[tc]], tag=f"addx{g}_{tc}")
            P.op("gp", (lambda e, sem, s1=s1, j1=j1, tc=tc: e.dma_start(
                out[s1, NT * j1 + 128 * tc:NT * j1 + 128 * tc + 128, :],
                XE[:, 1024 * tc:1024 * tc + 1024]).then_inc(sem, 16)),
                reads=[bXE4[tc]], tag=f"st{g}_{tc}")

        def attn_qb(g, qb, wo_s, filler):
            p, pn = g % 2, (g + 1) % 2
            j = g % TILES_PER_SEQ
            have_c1 = (qb < 3) or (j < 3 and g + 1 < n_iters)
            for ci in range(2 if have_c1 else 1):
                kc = qb + ci
                kpar, kcc = (p, kc) if kc < TCN else (pn, 0)
                def skfn(e, kpar=kpar, kcc=kcc, qb=qb):
                    last = None
                    for par in range(2):
                        r0 = 64 * par
                        for h in range(par, 16, 2):
                            ecb = 512 * (h // 2)
                            last = e.matmul(
                                psA[:, 128 * h:128 * h + 128],
                                KF[kpar][r0:r0 + 64, ecb + 128 * kcc:ecb + 128 * kcc + 128],
                                QF[r0:r0 + 64, ecb + 128 * qb:ecb + 128 * qb + 128],
                                start=True, stop=True)
                    return last
                P.op("pe", skfn, reads=[bKF[kpar], bQF],
                     writes=[bpsA[0], bpsA[1], bpsA[2], bpsA[3]],
                     tag=f"qk{g}_{qb}_{ci}_0")
                P.op("act", lambda e: e.activation(
                    st(200), CST[:, 0:1], AF.Exp),
                    reads=[bCST], writes=[bJNK], tag=f"dxp{g}_{qb}_{ci}")
                for b in range(4):
                    P.op("act", (lambda e, b=b, ci=ci: e.activation(
                        PT[:, 2048 * ci + 512 * b:2048 * ci + 512 * b + 512],
                        psA[:, 512 * b:512 * b + 512], AF.Exp)),
                        reads=[bpsA[b]], writes=[bPT[ci][b]],
                        tag=f"exp{g}_{qb}_{ci}_{b}")
                    P.op("dve", (lambda e, b=b, ci=ci: e.tensor_tensor(
                        PT[:, 2048 * ci + 512 * b:2048 * ci + 512 * b + 512],
                        PT[:, 2048 * ci + 512 * b:2048 * ci + 512 * b + 512],
                        EB[:, 2048 * ci + 512 * b:2048 * ci + 512 * b + 512],
                        ALU.mult)),
                        reads=[bEB, bPT[ci][b]], writes=[bPT[ci][b]],
                        tag=f"emul{g}_{qb}_{ci}_{b}")
                filler(1)

            def avfn(e, qb=qb, have_c1=have_c1, p=p, g=g):
                last = None
                for h in range(16):
                    for ci in range(2 if have_c1 else 1):
                        kc = qb + ci
                        vpar, vcc = (p, kc) if kc < TCN else ((g + 1) % 2, 0)
                        last = e.matmul(
                            psA[:, 128 * h:128 * h + 65],
                            PT[:, 2048 * ci + 128 * h:2048 * ci + 128 * h + 128],
                            VP[vpar][:, 1040 * vcc + 65 * h:1040 * vcc + 65 * h + 65],
                            start=(ci == 0),
                            stop=(ci == (1 if have_c1 else 0)))
                return last
            rd = [bPT[0][b] for b in range(4)] + [bVP[p]]
            if have_c1:
                rd += [bPT[1][b] for b in range(4)]
                rd.append(bVP[(g + 1) % 2] if qb == 3 else bVP[p])
            P.op("pe", avfn, reads=rd,
                 writes=[bpsA[0], bpsA[1], bpsA[2], bpsA[3]],
                 tag=f"av{g}_{qb}_0")

            P.op("dve", lambda e: e.reciprocal(
                st(RZ0, 16).rearrange("p (h u) -> p h u", u=1),
                psA[:, :].rearrange("p (h c) -> p h c", h=16)[:, :, 64:65]),
                reads=[bpsA[0], bpsA[1], bpsA[2], bpsA[3]], writes=[bRZ],
                tag=f"rz{g}_{qb}")
            P.op("dve", (lambda e, qb=qb: e.tensor_tensor(
                TM[:, 1024 * qb:1024 * qb + 1024].rearrange("p (h c) -> p h c", h=16),
                psA[:, :].rearrange("p (h c) -> p h c", h=16)[:, :, 0:64],
                st(RZ0, 16).unsqueeze(2).broadcast_to([128, 16, 64]), ALU.mult)),
                reads=[bpsA[0], bpsA[1], bpsA[2], bpsA[3], bRZ],
                writes=[bTM4[qb]], tag=f"aonrm{g}_{qb}")
            filler(1)

            def aotfn(e, qb=qb):
                last = None
                bc = psA[:, 0:512].bitcast(BF16)
                for ec in range(DCN):
                    last = e.transpose(
                        bc[:, 128 * ec:128 * ec + 128],
                        TM[:, 1024 * qb + 128 * ec:1024 * qb + 128 * ec + 128],
                        IDT[:, :])
                return last
            P.op("pe", aotfn, reads=[bTM4[qb], bIDT], writes=[bpsA[0]],
                 tag=f"aoT{g}_{qb}")
            P.op("act", lambda e: e.copy(AOF[:, :], psA[:, 0:512].bitcast(BF16)),
                 reads=[bpsA[0]], writes=[bAOF], tag=f"aoEv{g}_{qb}")

            for half in range(2):
                wap, wab = wo_s[half]

                def wofn(e, wap=wap, half=half):
                    last = None
                    for ec in range(DCN):
                        last = e.matmul(
                            psA[:, 1024 + 512 * half:1024 + 512 * half + 512],
                            AOF[:, 128 * ec:128 * ec + 128],
                            wap[:, 512 * ec:512 * ec + 512],
                            start=(ec == 0), stop=(ec == DCN - 1))
                    return last
                P.op("pe", wofn, reads=[bAOF, wab], writes=[bpsA[2 + half]],
                     tag=f"wo{g}_{qb}_{half}")
                P.op("act", (lambda e, qb=qb, half=half: e.copy(
                    SBUF_S[:, 1024 * qb + 512 * half:1024 * qb + 512 * half + 512],
                    psA[:, 1024 + 512 * half:1024 + 512 * half + 512])),
                    reads=[bpsA[2 + half]], writes=[bSB4[qb]],
                    tag=f"sEv{g}_{qb}_{half}")

        def tile(g, wq_s, prep_q=None):
            p, pn = g % 2, (g + 1) % 2
            fstate = [0]
            pf = []
            if prep_q is None:
                for hc in range(2):
                    pf.append(load_gu_hc("wg", hc, f"wg{g}_{hc}"))
                    pf.append(load_gu_hc("wu", hc, f"wu{g}_{hc}"))

            def filler(n):
                for _ in range(n):
                    if prep_q is not None:
                        if prep_q:
                            prep_q.pop(0)()
                    elif fstate[0] < HCN:
                        ffn_unit(g, fstate[0], pf)
                        fstate[0] += 1

            load_xe(g, f"ldxe{g}")

            for eh in range(2):
                for tc in proj_eh(g, eh, wq_s[eh], p, "qproj"):
                    norm_apply(g, "q", eh, tc)
                filler(1 + eh)
            kq_transpose(g, QF, bQF, "qT", range(TCN))
            filler(1)
            has_ln = g + 2 < n_iters
            if has_ln:
                ln_stats(g + 2)

            if g + 1 < n_iters:
                wk_s = load_wa("wk", 0, f"wk0_{g}"), load_wa("wk", 1, f"wk1_{g}")
                for eh in range(2):
                    for tc in proj_eh(g + 1, eh, wk_s[eh], pn, "kproj"):
                        norm_apply(g + 1, "k", eh, tc)
                    filler(1 + eh)
                kq_transpose(g + 1, KF[pn], bKF[pn], "kT", range(TCN))
                filler(1)
                wv_s = load_wa("wv", 0, f"wv0_{g}"), load_wa("wv", 1, f"wv1_{g}")
                for eh in range(2):
                    for tc in proj_eh(g + 1, eh, wv_s[eh], pn, "vproj"):
                        P.op("act", (lambda e, eh=eh, tc=tc, pn=pn: e.copy(
                            VP[pn][:, :].rearrange("p (kc h c) -> p kc h c", kc=4, c=65)[:, tc, 8 * eh:8 * eh + 8, 0:64],
                            psA[:, 512 * tc:512 * tc + 512].rearrange(
                                "p (h c) -> p h c", h=8))),
                            reads=[bpsA[tc]], writes=[bVP[pn]],
                            tag=f"vEv{g}_{eh}_{tc}")
                    filler(1)
            else:
                filler(6)

            if dbg and g == 0:
                for nm, src_ap, src_b in [("d_xn", XN[0], bXN[0][3]),
                                          ("d_kf", KF[1], bKF[1]),
                                          ("d_vp", VP[1], bVP[1]),
                                          ("d_qf", QF, bQF)]:
                    P.op("gp", (lambda e, sem, nm=nm, src_ap=src_ap: e.dma_start(
                        dbg_outs[nm], src_ap[:, :]).then_inc(sem, 16)),
                        reads=[src_b], tag=f"dbg_{nm}")

            nwq0 = load_wa("wq", 0, f"wq0_{g + 1}") if g + 1 < n_iters else None

            wo_s = load_wa("wo", 0, f"wo0_{g}"), load_wa("wo", 1, f"wo1_{g}")
            for qb in range(TCN):
                attn_qb(g, qb, wo_s, filler)

            wa_i[0] += 1
            nwq1 = load_wa("wq", 1, f"wq1_{g + 1}") if g + 1 < n_iters else None

            if dbg and g == 0:
                P.op("gp", (lambda e, sem: e.dma_start(
                    dbg_outs["d_pt"], PT[:, :]).then_inc(sem, 16)),
                    reads=[bPT[0][0], bPT[0][1], bPT[0][2], bPT[0][3],
                           bPT[1][0], bPT[1][1], bPT[1][2], bPT[1][3]],
                    tag="dbg_pt")
                P.op("gp", (lambda e, sem: e.dma_start(
                    dbg_outs["d_sb"], SBUF_S[:, :]).then_inc(sem, 16)),
                    reads=[bSB4[0], bSB4[1], bSB4[2], bSB4[3]], tag="dbg_sb")

            if has_ln:
                ln_apply(g + 2)
            if prep_q is not None:
                while prep_q:
                    prep_q.pop(0)()
                for hc in range(2):
                    pf.append(load_gu_hc("wg", hc, f"wg{g}_{hc}"))
                    pf.append(load_gu_hc("wu", hc, f"wu{g}_{hc}"))
                while fstate[0] < HCN:
                    ffn_unit(g, fstate[0], pf)
                    fstate[0] += 1
            else:
                filler(2)
                while fstate[0] < HCN:
                    filler(1)
            if has_ln:
                xn_T(g + 2, range(TCN))

            if dbg and g == 0:
                P.op("gp", (lambda e, sem: e.dma_start(
                    dbg_outs["d_hid"], HID[:, :]).then_inc(sem, 16)),
                    reads=bHID12, tag="dbg_hid")

            for hc2 in range(12):
                down_group(g, hc2)
                if hc2 == 4 and g + 3 < n_iters:
                    load_x(g + 3, f"ldx{g + 3}")
            epilogue_tc(g, 0)
            epilogue_tc(g, 1)
            epilogue_tc(g, 2)
            epilogue_tc(g, 3)
            return (nwq0, nwq1)

        P.op("sp", lambda e, sem: e.dma_start(
            EB[:, :].rearrange("p (c n) -> p c n", c=2),
            eb_in.rearrange("c p n -> p c n")).then_inc(sem, 16),
            writes=[bEB], tag="ld_eb")
        P.op("sp", lambda e, sem: e.dma_start(IDT[:, :], id_in).then_inc(sem, 16),
             writes=[bIDT], tag="ld_id")

        def cst_fn(e):
            e.memset(CST[:, 0:1], EPS_LN)
            e.memset(CST[:, 1:2], C127)
            e.memset(CST[:, 2:3], C127SQ)
            return e.memset(CST[:, 3:4], -0.5)
        P.op("dve", cst_fn, writes=[bCST], tag="cst")
        for i in range(2):
            P.op("dve", (lambda e, i=i: e.memset(
                VP[i][:, :].rearrange("p (kc h c) -> p kc h c", kc=4, c=65)[:, :, :, 64:65],
                1.0)), writes=[bVP[i]], tag=f"ones{i}")

        load_x(0, "ldx0")
        prep_attn("wq")
        prep_attn("wk")
        prep_attn("wv")
        prep_attn("wo")
        prep_wd()
        ln_stats(0)
        ln_apply(0)
        xn_T(0, range(TCN))
        load_x(1, "ldx1")
        ln_stats(1)
        ln_apply(1)
        xn_T(1, range(TCN))
        load_x(2, "ldx2")
        wk_s = load_wa("wk", 0, "wk0_p"), load_wa("wk", 1, "wk1_p")
        for eh in range(2):
            for tc in proj_eh(0, eh, wk_s[eh], 0, "kproj"):
                norm_apply(0, "k", eh, tc)
        kq_transpose(0, KF[0], bKF[0], "kT", range(TCN))
        wv_s = load_wa("wv", 0, "wv0_p"), load_wa("wv", 1, "wv1_p")
        for eh in range(2):
            for tc in proj_eh(0, eh, wv_s[eh], 0, "vproj"):
                P.op("act", (lambda e, eh=eh, tc=tc: e.copy(
                    VP[0][:, :].rearrange("p (kc h c) -> p kc h c", kc=4, c=65)[:, tc, 8 * eh:8 * eh + 8, 0:64],
                    psA[:, 512 * tc:512 * tc + 512].rearrange(
                        "p (h c) -> p h c", h=8))),
                    reads=[bpsA[tc]], writes=[bVP[0]], tag=f"vEvp_{eh}_{tc}")
        pin0_pool = [(WD[0][:, 0:1024], bWD[0]), (WD[1][:, 0:1024], bWD[1])] + [
            (HID[:, 1024 * i:1024 * i + 1024], bHID12[i]) for i in range(6)]
        prep_q = []
        for q8 in range(8):
            for nm in ("wg", "wu"):
                prep_q.append(lambda nm=nm, q8=q8: prep_gu_chunk(
                    nm, q8, pin0_pool, ps=psB, bps=bpsB))

        wq_s = load_wa("wq", 0, "wq0_0"), load_wa("wq", 1, "wq1_0")
        for g in range(n_iters):
            wq_s = tile(g, wq_s, prep_q=prep_q if g == 0 else None)

        counts = P.finalize()

        def run(eng_obj, name):
            for (waits, fn, evt, tag, nd) in P.ops[name]:
                for w in waits:
                    eng_obj.wait_ge(sems[w.sem], w.val)
                if nd > 0:
                    fn(eng_obj, sems[evt.sem])
                else:
                    inst = fn(eng_obj)
                    if evt.used:
                        inst.then_inc(sems[evt.sem], 1)
            if name == "gp":
                for i in range(NDMASEM):
                    s_ = f"gp{i}"
                    if counts.get(s_, 0):
                        eng_obj.wait_ge(sems[s_], counts[s_])

        with nc.Block() as block:
            @block.tensor
            def _(t):
                run(t, "pe")

            @block.scalar
            def _(a):
                run(a, "act")

            @block.vector
            def _(v):
                run(v, "dve")

            @block.sync
            def _(s):
                run(s, "sp")

            @block.gpsimd
            def _(gp):
                run(gp, "gp")

    return nc


_EB_NP, _ID_NP = None, None


def kernel(**inputs):
    global _EB_NP, _ID_NP
    if _EB_NP is None:
        _EB_NP, _ID_NP = _make_consts()

    x = np.ascontiguousarray(np.asarray(inputs["L0_attn_x"], dtype=np.float32))
    ws = {
        "wq": np.ascontiguousarray(np.asarray(inputs["L0_attn_Wq"], dtype=np.float32)),
        "wk": np.ascontiguousarray(np.asarray(inputs["L0_attn_Wk"], dtype=np.float32)),
        "wv": np.ascontiguousarray(np.asarray(inputs["L0_attn_Wv"], dtype=np.float32)),
        "wo": np.ascontiguousarray(np.asarray(inputs["L0_attn_Wo"], dtype=np.float32)),
        "wg": np.ascontiguousarray(np.asarray(inputs["L0_ffn_w_gate"], dtype=np.int32)),
        "wu": np.ascontiguousarray(np.asarray(inputs["L0_ffn_w_up"], dtype=np.int32)),
        "wd": np.ascontiguousarray(np.asarray(inputs["L0_ffn_w_down"], dtype=np.int32)),
    }
    nc = build_nc(NTILES, dbg=False)
    in_maps = []
    for c in range(NCORES):
        m = {"x": x[SEQ_PER_CORE * c:SEQ_PER_CORE * (c + 1)],
             "eb": _EB_NP, "ident": _ID_NP}
        m.update(ws)
        in_maps.append(m)
    res = run_bass_kernel_spmd(nc, in_maps, core_ids=list(range(NCORES)))
    return np.concatenate([res.results[c]["out"] for c in range(NCORES)], axis=0)


# version 23
# speedup vs baseline: 1.0282x; 1.0282x over previous
import numpy as np
import ml_dtypes
import concourse.bass as bass
import concourse.mybir as mybir
from concourse.bass_utils import run_bass_kernel_spmd

F32 = mybir.dt.float32
BF16 = mybir.dt.bfloat16
I32 = mybir.dt.int32
AF = mybir.ActivationFunctionType
ALU = mybir.AluOpType
AX = mybir.AxisListType

B, T, D, H, HD, WIN, FF = 32, 2048, 1024, 16, 64, 128, 3072
NCORES = 8
SEQ_PER_CORE = B // NCORES
NT = 512
TILES_PER_SEQ = T // NT
NTILES = SEQ_PER_CORE * TILES_PER_SEQ
TCN = NT // 128
DCN = D // 128
HCN = FF // 128
EPS_LN = 1e-5
EPS_L2 = 1e-6
C127 = float(np.float32(1.0 / 127.0))
C127SQ = float(np.float32(1.0 / 16129.0))


class _Evt:
    __slots__ = ("sem", "ord", "used", "val")

    def __init__(self, sem, ordn):
        self.sem, self.ord, self.used, self.val = sem, ordn, False, None


class _Buf:
    __slots__ = ("name", "ready", "users")

    def __init__(self, name):
        self.name, self.ready, self.users = name, None, []


_ENG_SEM = {"pe": "pe", "act": "act", "dve": "dve", "sp": "sp", "gp": "gp"}
_DMA_ENGS = ("sp", "gp")
NDMASEM = 6


class _Plan:
    def __init__(self):
        self.ops = {e: [] for e in _ENG_SEM}
        self.nsem = {}
        self.marks = {e: {} for e in _ENG_SEM}
        self.dmaidx = {e: 0 for e in _DMA_ENGS}
        self.last_on_sem = {}

    def op(self, eng, fn, reads=(), writes=(), tag="", ndma=1):
        waits = {}

        def need(evt):
            if evt is None:
                return
            m = self.marks[eng].get(evt.sem)
            if m is not None and m >= evt.ord:
                return
            cur = waits.get(evt.sem)
            if cur is None or evt.ord > cur.ord:
                waits[evt.sem] = evt

        for b in reads:
            need(b.ready)
        for b in writes:
            need(b.ready)
            for u in b.users:
                need(u)
        if eng in _DMA_ENGS and ndma > 0:
            slot = self.dmaidx[eng] % NDMASEM
            self.dmaidx[eng] += 1
            sem = f"{eng}{slot}"
            need(self.last_on_sem.get(sem))
        elif eng == "gp":
            sem = "gpc"
        else:
            sem = _ENG_SEM[eng]
        for s_, evt in waits.items():
            evt.used = True
            self.marks[eng][s_] = evt.ord
        ordn = self.nsem.get(sem, 0)
        self.nsem[sem] = ordn + 1
        evt = _Evt(sem, ordn)
        if eng in _DMA_ENGS and ndma > 0:
            evt.used = True
            self.last_on_sem[sem] = evt
        self.ops[eng].append((list(waits.values()), fn, evt, tag,
                              ndma if eng in _DMA_ENGS else 0))
        for b in writes:
            b.ready = evt
            b.users = []
        for b in reads:
            b.users.append(evt)
        return evt

    def finalize(self):
        cnt = {}
        for e in _ENG_SEM:
            for (_w, _f, evt, _t, nd) in self.ops[e]:
                if evt.used:
                    inc = 16 * nd if nd > 0 else 1
                    c = cnt.get(evt.sem, 0) + inc
                    cnt[evt.sem] = c
                    evt.val = c
        return cnt


def _make_consts():
    slopes = (2.0 ** (-(8.0 * (np.arange(H) / 15.0)))).astype(np.float64)
    k = np.arange(128)[:, None]
    q = np.arange(128)[None, :]
    eb = np.zeros((2, 128, H * 128), np.float64)
    rel0 = k - q
    rel1 = 128 + k - q
    for h in range(H):
        e0 = np.where(rel0 >= 0, np.exp(-np.abs(rel0) * slopes[h]), 0.0)
        e1 = np.where(rel1 <= 127, np.exp(-np.abs(rel1) * slopes[h]), 0.0)
        eb[0][:, 128 * h:128 * h + 128] = e0
        eb[1][:, 128 * h:128 * h + 128] = e1
    return eb.astype(ml_dtypes.bfloat16), np.eye(128, dtype=ml_dtypes.bfloat16)


def build_nc(n_iters=NTILES, dbg=False):
    nc = bass.Bass("TRN2", target_bir_lowering=False)

    x_in = nc.dram_tensor("x", [SEQ_PER_CORE, T, D], F32, kind="ExternalInput").ap()
    w_in = {}
    for nm, shp, dt in [("wq", [D, D], F32), ("wk", [D, D], F32),
                        ("wv", [D, D], F32), ("wo", [D, D], F32),
                        ("wg", [FF, D], I32), ("wu", [FF, D], I32),
                        ("wd", [D, FF], I32)]:
        w_in[nm] = nc.dram_tensor(nm, shp, dt, kind="ExternalInput").ap()
    eb_in = nc.dram_tensor("eb", [2, 128, H * 128], BF16, kind="ExternalInput").ap()
    id_in = nc.dram_tensor("ident", [128, 128], BF16, kind="ExternalInput").ap()
    out = nc.dram_tensor("out", [SEQ_PER_CORE, T, D], F32, kind="ExternalOutput").ap()

    wT = {nm: nc.dram_tensor(nm + "T_s", [2, 128, 4096], BF16, kind="Internal").ap()
          for nm in ("wq", "wk", "wv", "wo")}
    wT["wg"] = nc.dram_tensor("wgT_s", [24, 128, 1024], BF16, kind="Internal").ap()
    wT["wu"] = nc.dram_tensor("wuT_s", [24, 128, 1024], BF16, kind="Internal").ap()
    wT["wd"] = nc.dram_tensor("wdT_s", [24, 128, 1024], BF16, kind="Internal").ap()

    dbg_outs = {}
    if dbg:
        for nm, shp, dt in [
            ("d_xn", [128, 4096], BF16), ("d_kf", [128, 4096], BF16),
            ("d_vp", [128, 4160], BF16), ("d_qf", [128, 4096], BF16),
            ("d_pt", [128, 4096], BF16), ("d_sb", [128, 4096], BF16),
            ("d_hid", [128, 12288], BF16),
        ]:
            dbg_outs[nm] = nc.dram_tensor(nm, shp, dt, kind="ExternalOutput").ap()

    P = _Plan()
    import contextlib
    stack = contextlib.ExitStack()
    sb = lambda name, shape, dt: stack.enter_context(nc.sbuf_tensor(name, shape, dt))
    ps = lambda name, shape, dt: stack.enter_context(nc.psum_tensor(name, shape, dt))
    mksem = lambda name: stack.enter_context(nc.semaphore(name))

    with stack:
        XL = sb("XL", [128, 4096], F32)
        XE = sb("XE", [128, 4096], F32)
        XN = [sb(f"XN{i}", [128, 4096], BF16) for i in range(2)]
        TM = sb("TM", [128, 4096], BF16)
        QF = sb("QF", [128, 4096], BF16)
        KF = [sb(f"KF{i}", [128, 4096], BF16) for i in range(2)]
        VP = [sb(f"VP{i}", [128, 4160], BF16) for i in range(2)]
        PT = sb("PT", [128, 4096], BF16)
        SBUF_S = sb("SBS", [128, 4096], BF16)
        HID = sb("HID", [128, 12288], BF16)
        GBUF = sb("GBUF", [128, 1024], F32)
        AOF = sb("AOF", [128, 1024], BF16)
        SQ = sb("SQ", [128, 1024], F32)
        TSC = sb("TSC", [128, 1024], BF16)
        EB = sb("EB", [128, 4096], BF16)
        IDT = sb("IDT", [128, 128], BF16)
        CST = sb("CST", [128, 4], F32)
        STAT = sb("STAT", [128, 256], F32)
        WA = [sb(f"WA{i}", [128, 4096], BF16) for i in range(3)]
        WGU = [sb(f"WGU{i}", [128, 3072], BF16) for i in range(3)]
        WD = [sb(f"WD{i}", [128, 2048], BF16) for i in range(2)]

        psA = ps("psA", [128, 2048], F32)
        psB = ps("psB", [128, 2048], F32)

        sems = {n: mksem(n) for n in ("pe", "act", "dve", "gpc")}
        for e in _DMA_ENGS:
            for i in range(NDMASEM):
                sems[f"{e}{i}"] = mksem(f"{e}{i}")

        bXL, bQF, bAOF = (_Buf(n) for n in ("XL", "QF", "AOF"))
        bXE4 = [_Buf(f"XE{t}") for t in range(TCN)]
        bTM4 = [_Buf(f"TM{i}") for i in range(TCN)]
        bXN = [[_Buf(f"XN{i}_{t}") for t in range(TCN)] for i in range(2)]
        bKF = [_Buf(f"KF{i}") for i in range(2)]
        bVP = [_Buf(f"VP{i}") for i in range(2)]
        bPT = [[_Buf(f"PT{c}_{b}") for b in range(4)] for c in range(2)]
        bSB4 = [_Buf(f"SB{q}") for q in range(TCN)]
        bHID12 = [_Buf(f"HID{i}") for i in range(12)]
        bG = _Buf("G")
        bSQ2 = [_Buf(f"SQ{i}") for i in range(2)]
        bTSC = [_Buf(f"TSC{i}") for i in range(2)]
        bEB, bIDT, bCST = _Buf("EB"), _Buf("IDT"), _Buf("CST")
        bNRM = [_Buf(f"NRM{i}") for i in range(8)]
        bJNK = _Buf("JNK")
        bQF4 = [_Buf(f"QFs{i}") for i in range(4)]
        bPT2 = [_Buf(f"PTs{i}") for i in range(2)]
        bRZ = _Buf("RZ")
        bpsA = [_Buf(f"psA{b}") for b in range(4)]
        bpsB = [_Buf(f"psB{b}") for b in range(4)]
        bWT = {n: _Buf("wt_" + n) for n in ("wq", "wk", "wv", "wo", "wg", "wu", "wd")}
        bWA = [_Buf(f"WA{i}") for i in range(3)]
        bWGU = [_Buf(f"WGU{i}") for i in range(3)]
        bWD = [_Buf(f"WD{i}") for i in range(2)]

        LNS4, LNSS4, LNM4, LNMSQ4, LNV4, LNR4, LNMR4 = (4 * i for i in range(7))
        RZ0 = 160
        st = lambda c, w=1: STAT[:, c:c + w]
        bLN1 = _Buf("LN1")

        wa_i, gu_i, wd_i = [0], [0], [0]

        def load_wa(nm, eh, tag):
            s = wa_i[0] % 3
            wa_i[0] += 1
            ap, bf = WA[s], bWA[s]
            src_ap = wT[nm][eh]
            P.op("sp", (lambda e, sem, ap=ap, src_ap=src_ap:
                        e.dma_start(ap[:, :], src_ap).then_inc(sem, 16)),
                 reads=[bWT[nm]], writes=[bf], tag=tag)
            return ap, bf

        gu9_i = [0]
        bGU9 = [_Buf(f"GU9_{i}") for i in range(9)]

        def load_gu_hc(nm, hc, tag):
            s = gu9_i[0] % 9
            gu9_i[0] += 1
            ap, off = WGU[s // 3], 1024 * (s % 3)
            bf = bGU9[s]
            P.op("sp", (lambda e, sem, ap=ap, off=off, src=wT[nm][hc]:
                        e.dma_start(ap[:, off:off + 1024], src).then_inc(sem, 16)),
                 reads=[bWT[nm]], writes=[bf, bWGU[s // 3]], tag=tag)
            return (ap, off), bf

        def load_wd2(hc2, tag):
            s = wd_i[0] % 2
            wd_i[0] += 1
            ap, bf = WD[s], bWD[s]
            P.op("sp", (lambda e, sem, ap=ap, hc2=hc2: e.dma_start(
                ap[:, :].rearrange("p (i q) -> p i q", i=2),
                wT["wd"][2 * hc2:2 * hc2 + 2].rearrange("i p q -> p i q")
                ).then_inc(sem, 16)),
                 reads=[bWT["wd"]], writes=[bf], tag=tag)
            return ap, bf

        pin_pool = [(WD[0][:, 0:1024], bWD[0]), (WD[1][:, 0:1024], bWD[1])] + [
            (QF[:, 1024 * i:1024 * i + 1024], bQF4[i]) for i in range(4)]
        pinw_pool = [(WGU[i][:, 0:1536], bWGU[i]) for i in range(3)] + [
            (PT[:, 2048 * i:2048 * i + 1536], bPT2[i]) for i in range(2)]

        def _prep_in(nm, rows0, ncols, pool, idx, cols0=0):
            s = idx[0] % len(pool)
            q = "gp"
            idx[0] += 1
            ap, bf = pool[s]
            P.op(q, (lambda e, sem, ap=ap, nm=nm, rows0=rows0, ncols=ncols,
                     cols0=cols0:
                     e.dma_start(ap[:, 0:ncols],
                                 w_in[nm][rows0:rows0 + 128,
                                          cols0:cols0 + ncols]
                                 ).then_inc(sem, 16)),
                 writes=[bf], tag=f"pin_{nm}_{rows0}_{cols0}")
            return ap, bf

        def _prep_T(inap, inbf, nblk, bank, tag, ps=None, bps=None):
            ps = psA if ps is None else ps
            bps = bpsA if bps is None else bps
            def fn(e, inap=inap, nblk=nblk, bank=bank, ps=ps):
                last = None
                bc = ps[:, 512 * bank:512 * bank + 512].bitcast(BF16)
                if nblk > 8:
                    bc = ps[:, 512 * bank:512 * bank + 1024].bitcast(BF16)
                for i in range(nblk):
                    last = e.transpose(bc[:, 128 * i:128 * i + 128],
                                       inap[:, 128 * i:128 * i + 128], IDT[:, :])
                return last
            wrts = [bps[bank]] + ([bps[bank + 1]] if nblk > 8 else [])
            P.op("pe", fn, reads=[inbf, bIDT], writes=wrts, tag=tag)
            return wrts

        def prep_attn(nm):
            for eh in range(2):
                s = wa_i[0] % 3
                wa_i[0] += 1
                stg, bstg = WA[s], bWA[s]
                for r in range(4):
                    inap, inbf = _prep_in(nm, 512 * eh + 128 * r, 1024,
                                          pin_pool, wd_i)
                    pb = _prep_T(inap, inbf, 8, r % 4, f"pT_{nm}{eh}_{r}")
                    P.op("act", (lambda e, stg=stg, r=r, bank=r % 4: e.copy(
                        stg[:, :].rearrange("p (dc q) -> p dc q", q=512)[:, :, 128 * r:128 * r + 128],
                        psA[:, 512 * bank:512 * bank + 512].bitcast(BF16).rearrange(
                            "p (dc q) -> p dc q", dc=8))),
                        reads=pb, writes=[bstg], tag=f"pEv_{nm}{eh}_{r}")
                P.op("sp", (lambda e, sem, stg=stg, nm=nm, eh=eh:
                            e.dma_start(wT[nm][eh], stg[:, :]).then_inc(sem, 16)),
                     reads=[bstg], writes=[bWT[nm]], tag=f"pS_{nm}{eh}")

        def prep_gu_chunk(nm, q8, pool, ps=None, bps=None):
            pps = psA if ps is None else ps
            s = gu_i[0] % 3
            gu_i[0] += 1
            stg, bstg = WGU[s], bWGU[s]
            for j in range(3):
                inap, inbf = _prep_in(nm, 384 * q8 + 128 * j, 1024,
                                      pool, wd_i)
                pb = _prep_T(inap, inbf, 8, j % 4, f"pT_{nm}{q8}_{j}",
                             ps=ps, bps=bps)
                P.op("act", (lambda e, stg=stg, j=j, bank=j % 4, pps=pps: e.copy(
                    stg[:, :].rearrange("p (dc q) -> p dc q", q=384)[:, :, 128 * j:128 * j + 128],
                    pps[:, 512 * bank:512 * bank + 512].bitcast(BF16).rearrange(
                        "p (dc q) -> p dc q", dc=8))),
                    reads=pb, writes=[bstg], tag=f"pEv_{nm}{q8}_{j}")
            for j in range(3):
                P.op("sp", (lambda e, sem, stg=stg, nm=nm, q8=q8, j=j:
                            e.dma_start(
                    wT[nm][3 * q8 + j].rearrange("p (dc e) -> p dc e", dc=8),
                    stg[:, :].rearrange("p (dc j e) -> p j dc e",
                                        j=3, e=128)[:, j]).then_inc(sem, 16)),
                     reads=[bstg], writes=[bWT[nm]], tag=f"pS_{nm}{q8}_{j}")

        def prep_wd():
            for half in range(2):
                for r in range(8):
                    inap, inbf = _prep_in("wd", 128 * r, 1536, pinw_pool, gu_i,
                                          cols0=1536 * half)
                    pb = _prep_T(inap, inbf, 12, 2 * (r % 2), f"pT_wd{half}_{r}")
                    P.op("act", (lambda e, r=r, bank=2 * (r % 2): e.copy(
                        HID[:, :].rearrange("p (hc q) -> p hc q", q=1024)[:, 0:12, 128 * r:128 * r + 128],
                        psA[:, 512 * bank:512 * bank + 1024].bitcast(BF16)[:, 0:1536].rearrange(
                            "p (hc q) -> p hc q", hc=12))),
                        reads=pb, writes=bHID12, tag=f"pEv_wd{half}_{r}")
                for i in range(12):
                    P.op("sp", (lambda e, sem, i=i, half=half:
                                e.dma_start(wT["wd"][12 * half + i],
                                            HID[:, 1024 * i:1024 * i + 1024]
                                            ).then_inc(sem, 16)),
                         reads=bHID12, writes=[bWT["wd"]], tag=f"pS_wd{half}_{i}")

        def load_x(g, tag):
            s1, j1 = divmod(g, TILES_PER_SEQ)
            src = x_in[s1, NT * j1:NT * (j1 + 1), :].rearrange(
                "(tc tp) d -> tp tc d", tp=128)
            P.op("gp", (lambda e, sem, src=src: e.dma_start(
                XL[:, :].rearrange("p (tc d) -> p tc d", tc=TCN),
                src).then_inc(sem, 16)), writes=[bXL], tag=tag)

        def load_xe(g, tag):
            s1, j1 = divmod(g, TILES_PER_SEQ)
            src = x_in[s1, NT * j1:NT * (j1 + 1), :].rearrange(
                "(tc tp) d -> tp tc d", tp=128)
            P.op("gp", (lambda e, sem, src=src: e.dma_start(
                XE[:, :].rearrange("p (tc d) -> p tc d", tc=TCN),
                src).then_inc(sem, 16)), writes=bXE4, tag=tag)

        def ln_stats(gt):
            for tc in range(TCN):
                xsl = XL[:, 1024 * tc:1024 * (tc + 1)]
                P.op("act", (lambda e, xsl=xsl, tc=tc: e.activation(
                    GBUF[:, :], xsl, AF.Copy,
                    accum_out=st(LNS4 + tc))),
                    reads=[bXL], writes=[bG, bLN1], tag=f"lnsum{gt}_{tc}")
                P.op("act", (lambda e, xsl=xsl, tc=tc: e.activation(
                    GBUF[:, :], xsl, AF.Square,
                    accum_out=st(LNSS4 + tc))),
                    reads=[bXL], writes=[bG, bLN1], tag=f"lnss{gt}_{tc}")
            P.op("gp", lambda e: e.tensor_scalar_mul(
                st(LNM4, 4), st(LNS4, 4), -1.0 / D),
                reads=[bLN1], writes=[bLN1], ndma=0)
            P.op("gp", lambda e: e.tensor_mul(
                st(LNMSQ4, 4), st(LNM4, 4), st(LNM4, 4)),
                reads=[bLN1], writes=[bLN1], ndma=0)
            P.op("gp", lambda e: e.tensor_scalar_mul(
                st(LNV4, 4), st(LNSS4, 4), 1.0 / D),
                reads=[bLN1], writes=[bLN1], ndma=0)
            P.op("gp", lambda e: e.tensor_sub(
                st(LNV4, 4), st(LNV4, 4), st(LNMSQ4, 4)),
                reads=[bLN1], writes=[bLN1], ndma=0)
            P.op("gp", lambda e: e.tensor_scalar_add(
                st(LNV4, 4), st(LNV4, 4), EPS_LN),
                reads=[bLN1], writes=[bLN1], ndma=0)
            P.op("act", lambda e: e.activation(
                st(LNR4, 4), st(LNV4, 4), AF.Sqrt),
                reads=[bLN1], writes=[bLN1])
            P.op("dve", lambda e: e.reciprocal(st(LNR4, 4), st(LNR4, 4)),
                 reads=[bLN1], writes=[bLN1])
            P.op("gp", lambda e: e.tensor_mul(
                st(LNMR4, 4), st(LNM4, 4), st(LNR4, 4)),
                reads=[bLN1], writes=[bLN1], ndma=0)

        def ln_apply(gt):
            for tc in range(TCN):
                P.op("act", (lambda e, tc=tc: e.activation(
                    TM[:, 1024 * tc:1024 * (tc + 1)],
                    XL[:, 1024 * tc:1024 * (tc + 1)], AF.Copy,
                    scale=st(LNR4 + tc))),
                    reads=[bXL, bLN1], writes=[bTM4[tc]], tag=f"lnap{gt}_{tc}")
                P.op("dve", (lambda e, tc=tc: e.tensor_tensor(
                    TM[:, 1024 * tc:1024 * (tc + 1)],
                    TM[:, 1024 * tc:1024 * (tc + 1)],
                    st(LNMR4 + tc).broadcast_to([128, 1024]), ALU.add)),
                    reads=[bLN1, bTM4[tc]], writes=[bTM4[tc]],
                    tag=f"lnad{gt}_{tc}")

        def xn_T(gt, tcs):
            p1 = gt % 2
            for tc in tcs:
                def tfn(e, tc=tc):
                    last = None
                    bc = psA[:, 512 * tc:512 * tc + 512].bitcast(BF16)
                    for dc in range(DCN):
                        last = e.transpose(
                            bc[:, 128 * dc:128 * dc + 128],
                            TM[:, 1024 * tc + 128 * dc:1024 * tc + 128 * dc + 128],
                            IDT[:, :])
                    return last
                P.op("pe", tfn, reads=[bTM4[tc], bIDT], writes=[bpsA[tc]],
                     tag=f"xnT{gt}_{tc}")
                P.op("act", (lambda e, tc=tc, p1=p1: e.copy(
                    XN[p1][:, :].rearrange("p (dc t) -> p dc t", dc=DCN)[:, :, 128 * tc:128 * tc + 128],
                    psA[:, 512 * tc:512 * tc + 512].bitcast(BF16).rearrange(
                        "p (dc t) -> p dc t", dc=DCN))),
                    reads=[bpsA[tc]], writes=[bXN[p1][tc]], tag=f"xnEv{gt}_{tc}")

        def norm_apply(g1, which, eh, tc):
            r = 2 * tc + eh
            SS8, N8 = 32 + 16 * r, 40 + 16 * r
            bN = bNRM[r]
            sqo = 512 * (tc % 2)
            bS = bSQ2[tc % 2]
            P.op("act", (lambda e, tc=tc, sqo=sqo: e.activation(
                SQ[:, sqo:sqo + 512], psA[:, 512 * tc:512 * tc + 512], AF.Square)),
                reads=[bpsA[tc]], writes=[bS], tag=f"{which}sq{g1}_{eh}_{tc}")
            P.op("dve", (lambda e, SS8=SS8, sqo=sqo: e.tensor_reduce(
                st(SS8, 8).rearrange("p (h u) -> p h u", u=1),
                SQ[:, sqo:sqo + 512].rearrange("p (h c) -> p h c", h=8),
                AX.X, ALU.add)), reads=[bS], writes=[bN])
            P.op("act", (lambda e, SS8=SS8, N8=N8: e.activation(
                st(N8, 8), st(SS8, 8), AF.Sqrt)), reads=[bN], writes=[bN])
            P.op("dve", (lambda e, N8=N8: e.tensor_scalar_max(
                st(N8, 8), st(N8, 8), EPS_L2)), reads=[bN], writes=[bN])
            P.op("dve", (lambda e, N8=N8: e.reciprocal(st(N8, 8), st(N8, 8))),
                 reads=[bN], writes=[bN])
            P.op("dve", (lambda e, eh=eh, tc=tc, N8=N8: e.tensor_tensor(
                TM[:, 1024 * tc + 512 * eh:1024 * tc + 512 * eh + 512].rearrange(
                    "p (h c) -> p h c", h=8),
                psA[:, 512 * tc:512 * tc + 512].rearrange("p (h c) -> p h c", h=8),
                st(N8, 8).unsqueeze(2).broadcast_to([128, 8, 64]), ALU.mult)),
                reads=[bpsA[tc], bN], writes=[bTM4[tc]], tag=f"{which}nrm{g1}_{eh}_{tc}")

        def kq_transpose(g1, DST, bDST, tag, tcs):
            for tc in tcs:
                def tfn(e, tc=tc):
                    last = None
                    bc = psA[:, 512 * tc:512 * tc + 512].bitcast(BF16)
                    for ec in range(DCN):
                        last = e.transpose(
                            bc[:, 128 * ec:128 * ec + 128],
                            TM[:, 1024 * tc + 128 * ec:1024 * tc + 128 * ec + 128],
                            IDT[:, :])
                    return last
                P.op("pe", tfn, reads=[bTM4[tc], bIDT], writes=[bpsA[tc]],
                     tag=f"{tag}{g1}_{tc}")
                P.op("act", (lambda e, tc=tc, DST=DST: e.copy(
                    DST[:, :].rearrange("p (ec t) -> p ec t", ec=DCN)[:, :, 128 * tc:128 * tc + 128],
                    psA[:, 512 * tc:512 * tc + 512].bitcast(BF16).rearrange(
                        "p (ec t) -> p ec t", ec=DCN))),
                    reads=[bpsA[tc]], writes=[bDST], tag=f"{tag}Ev{g1}_{tc}")

        def proj_eh(g1, eh, wslot, p1, tag):
            wap, wab = wslot
            for tc in range(TCN):
                def fn(e, eh=eh, tc=tc, wap=wap, p1=p1):
                    last = None
                    for dc in range(DCN):
                        last = e.matmul(
                            psA[:, 512 * tc:512 * tc + 512],
                            XN[p1][:, 512 * dc + 128 * tc:512 * dc + 128 * tc + 128],
                            wap[:, 512 * dc:512 * dc + 512],
                            start=(dc == 0), stop=(dc == DCN - 1))
                    return last
                P.op("pe", fn, reads=[bXN[p1][tc], wab], writes=[bpsA[tc]],
                     tag=f"{tag}{g1}_{eh}_{tc}")
                yield tc

        def ffn_unit(g, hc, pf):
            p = g % 2
            (gap, goff), gab = pf.pop(0)
            (uap, uoff), uab = pf.pop(0)
            if hc + 2 < HCN:
                pf.append(load_gu_hc("wg", hc + 2, f"wg{g}_{hc + 2}"))
                pf.append(load_gu_hc("wu", hc + 2, f"wu{g}_{hc + 2}"))
            go = 512 * (hc % 2)
            uo = 1024 + 512 * (hc % 2)

            def gfn(e, gap=gap, goff=goff, p=p, go=go):
                last = None
                for dc in range(DCN):
                    last = e.matmul(
                        psB[:, go:go + 512],
                        gap[:, goff + 128 * dc:goff + 128 * dc + 128],
                        XN[p][:, 512 * dc:512 * dc + 512],
                        start=(dc == 0), stop=(dc == DCN - 1))
                return last
            P.op("pe", gfn, reads=bXN[p] + [gab], writes=[bpsB[hc % 2]],
                 tag=f"gate{g}_{hc}")
            P.op("act", (lambda e, hc=hc, go=go: e.activation(
                TSC[:, 512 * (hc % 2):512 * (hc % 2) + 512], psB[:, go:go + 512],
                AF.Silu, scale=CST[:, 1:2])),
                reads=[bpsB[hc % 2], bCST], writes=[bTSC[hc % 2]],
                tag=f"silu{g}_{hc}")

            def ufn(e, uap=uap, uoff=uoff, p=p, uo=uo):
                last = None
                for dc in range(DCN):
                    last = e.matmul(
                        psB[:, uo:uo + 512],
                        uap[:, uoff + 128 * dc:uoff + 128 * dc + 128],
                        XN[p][:, 512 * dc:512 * dc + 512],
                        start=(dc == 0), stop=(dc == DCN - 1))
                return last
            P.op("pe", ufn, reads=bXN[p] + [uab], writes=[bpsB[2 + hc % 2]],
                 tag=f"up{g}_{hc}")
            P.op("dve", (lambda e, hc=hc, uo=uo: e.tensor_tensor(
                HID[:, 512 * hc:512 * hc + 512], psB[:, uo:uo + 512],
                TSC[:, 512 * (hc % 2):512 * (hc % 2) + 512], ALU.mult)),
                reads=[bpsB[2 + hc % 2], bTSC[hc % 2]], writes=[bHID12[hc // 2]],
                tag=f"hid{g}_{hc}")

        _DFB = {0: (psA, 0), 1: (psA, 1024), 2: (psB, 0), 3: (psB, 1024)}

        def down_group(g, hc2):
            wdap, wdb_ = load_wd2(hc2, f"wd{g}_{hc2}")

            def dfn(e, hc2=hc2, wdap=wdap):
                last = None
                for i in range(2):
                    hc = 2 * hc2 + i
                    for tc in range(TCN):
                        pt, off = _DFB[tc]
                        for dh in range(2):
                            last = e.matmul(
                                pt[:, off + 512 * dh:off + 512 * dh + 512],
                                HID[:, 512 * hc + 128 * tc:512 * hc + 128 * tc + 128],
                                wdap[:, 1024 * i + 512 * dh:1024 * i + 512 * dh + 512],
                                start=(hc == 0), stop=(hc == HCN - 1))
                return last
            wr = [bpsA[0], bpsA[1], bpsA[2], bpsA[3],
                  bpsB[0], bpsB[1], bpsB[2], bpsB[3]]
            P.op("pe", dfn, reads=[bHID12[hc2], wdb_], writes=wr,
                 tag=f"down{g}_{hc2}")

        def epilogue_tc(g, tc):
            s1, j1 = divmod(g, TILES_PER_SEQ)
            pt, off = _DFB[tc]
            bp = bpsA if tc < 2 else bpsB
            P.op("act", (lambda e, pt=pt, off=off: e.activation(
                GBUF[:, :], pt[:, off:off + 1024], AF.Silu, scale=CST[:, 2:3])),
                reads=[bp[2 * (tc % 2)], bp[2 * (tc % 2) + 1], bCST],
                writes=[bG], tag=f"gsilu{g}_{tc}")
            P.op("dve", (lambda e, tc=tc: e.tensor_tensor(
                GBUF[:, :], GBUF[:, :],
                SBUF_S[:, 1024 * tc:1024 * tc + 1024], ALU.mult)),
                reads=[bG, bSB4[tc]], writes=[bG], tag=f"sg{g}_{tc}")
            P.op("dve", (lambda e, tc=tc: e.tensor_tensor(
                XE[:, 1024 * tc:1024 * tc + 1024],
                GBUF[:, :],
                XE[:, 1024 * tc:1024 * tc + 1024], ALU.add)),
                reads=[bG, bXE4[tc]], writes=[bXE4[tc]], tag=f"addx{g}_{tc}")
            P.op("gp", (lambda e, sem, s1=s1, j1=j1, tc=tc: e.dma_start(
                out[s1, NT * j1 + 128 * tc:NT * j1 + 128 * tc + 128, :],
                XE[:, 1024 * tc:1024 * tc + 1024]).then_inc(sem, 16)),
                reads=[bXE4[tc]], tag=f"st{g}_{tc}")

        def attn_qb(g, qb, wo_s, filler):
            p, pn = g % 2, (g + 1) % 2
            j = g % TILES_PER_SEQ
            have_c1 = (qb < 3) or (j < 3 and g + 1 < n_iters)
            for ci in range(2 if have_c1 else 1):
                kc = qb + ci
                kpar, kcc = (p, kc) if kc < TCN else (pn, 0)
                for par in range(2):
                    def skfn(e, par=par, kpar=kpar, kcc=kcc, qb=qb):
                        last = None
                        r0 = 64 * par
                        for h in range(par, 16, 2):
                            ecb = 512 * (h // 2)
                            last = e.matmul(
                                psA[:, 128 * h:128 * h + 128],
                                KF[kpar][r0:r0 + 64, ecb + 128 * kcc:ecb + 128 * kcc + 128],
                                QF[r0:r0 + 64, ecb + 128 * qb:ecb + 128 * qb + 128],
                                start=True, stop=True)
                        return last
                    P.op("pe", skfn, reads=[bKF[kpar], bQF],
                         writes=[bpsA[0], bpsA[1], bpsA[2], bpsA[3]],
                         tag=f"qk{g}_{qb}_{ci}_{par}")
                P.op("act", lambda e: e.activation(
                    st(200), CST[:, 0:1], AF.Exp),
                    reads=[bCST], writes=[bJNK], tag=f"dxp{g}_{qb}_{ci}")
                for b in range(4):
                    P.op("act", (lambda e, b=b, ci=ci: e.activation(
                        PT[:, 2048 * ci + 512 * b:2048 * ci + 512 * b + 512],
                        psA[:, 512 * b:512 * b + 512], AF.Exp)),
                        reads=[bpsA[b]], writes=[bPT[ci][b]],
                        tag=f"exp{g}_{qb}_{ci}_{b}")
                    P.op("dve", (lambda e, b=b, ci=ci: e.tensor_tensor(
                        PT[:, 2048 * ci + 512 * b:2048 * ci + 512 * b + 512],
                        PT[:, 2048 * ci + 512 * b:2048 * ci + 512 * b + 512],
                        EB[:, 2048 * ci + 512 * b:2048 * ci + 512 * b + 512],
                        ALU.mult)),
                        reads=[bEB, bPT[ci][b]], writes=[bPT[ci][b]],
                        tag=f"emul{g}_{qb}_{ci}_{b}")
                filler(1)

            for b in range(4):
                def avfn(e, b=b, qb=qb, have_c1=have_c1, p=p, g=g):
                    last = None
                    for h in range(4 * b, 4 * b + 4):
                        for ci in range(2 if have_c1 else 1):
                            kc = qb + ci
                            vpar, vcc = (p, kc) if kc < TCN else ((g + 1) % 2, 0)
                            last = e.matmul(
                                psA[:, 128 * h:128 * h + 65],
                                PT[:, 2048 * ci + 128 * h:2048 * ci + 128 * h + 128],
                                VP[vpar][:, 1040 * vcc + 65 * h:1040 * vcc + 65 * h + 65],
                                start=(ci == 0),
                                stop=(ci == (1 if have_c1 else 0)))
                    return last
                rd = [bPT[0][b], bVP[p]]
                if have_c1:
                    rd += [bPT[1][b], bVP[(g + 1) % 2] if qb == 3 else bVP[p]]
                P.op("pe", avfn, reads=rd, writes=[bpsA[b]],
                     tag=f"av{g}_{qb}_{b}")

            P.op("dve", lambda e: e.reciprocal(
                st(RZ0, 16).rearrange("p (h u) -> p h u", u=1),
                psA[:, :].rearrange("p (h c) -> p h c", h=16)[:, :, 64:65]),
                reads=[bpsA[0], bpsA[1], bpsA[2], bpsA[3]], writes=[bRZ],
                tag=f"rz{g}_{qb}")
            P.op("dve", (lambda e, qb=qb: e.tensor_tensor(
                TM[:, 1024 * qb:1024 * qb + 1024].rearrange("p (h c) -> p h c", h=16),
                psA[:, :].rearrange("p (h c) -> p h c", h=16)[:, :, 0:64],
                st(RZ0, 16).unsqueeze(2).broadcast_to([128, 16, 64]), ALU.mult)),
                reads=[bpsA[0], bpsA[1], bpsA[2], bpsA[3], bRZ],
                writes=[bTM4[qb]], tag=f"aonrm{g}_{qb}")
            filler(1)

            def aotfn(e, qb=qb):
                last = None
                bc = psA[:, 0:512].bitcast(BF16)
                for ec in range(DCN):
                    last = e.transpose(
                        bc[:, 128 * ec:128 * ec + 128],
                        TM[:, 1024 * qb + 128 * ec:1024 * qb + 128 * ec + 128],
                        IDT[:, :])
                return last
            P.op("pe", aotfn, reads=[bTM4[qb], bIDT], writes=[bpsA[0]],
                 tag=f"aoT{g}_{qb}")
            P.op("act", lambda e: e.copy(AOF[:, :], psA[:, 0:512].bitcast(BF16)),
                 reads=[bpsA[0]], writes=[bAOF], tag=f"aoEv{g}_{qb}")

            for half in range(2):
                wap, wab = wo_s[half]

                def wofn(e, wap=wap, half=half):
                    last = None
                    for ec in range(DCN):
                        last = e.matmul(
                            psA[:, 1024 + 512 * half:1024 + 512 * half + 512],
                            AOF[:, 128 * ec:128 * ec + 128],
                            wap[:, 512 * ec:512 * ec + 512],
                            start=(ec == 0), stop=(ec == DCN - 1))
                    return last
                P.op("pe", wofn, reads=[bAOF, wab], writes=[bpsA[2 + half]],
                     tag=f"wo{g}_{qb}_{half}")
                P.op("act", (lambda e, qb=qb, half=half: e.copy(
                    SBUF_S[:, 1024 * qb + 512 * half:1024 * qb + 512 * half + 512],
                    psA[:, 1024 + 512 * half:1024 + 512 * half + 512])),
                    reads=[bpsA[2 + half]], writes=[bSB4[qb]],
                    tag=f"sEv{g}_{qb}_{half}")

        def tile(g, wq_s, prep_q=None):
            p, pn = g % 2, (g + 1) % 2
            fstate = [0]
            pf = []
            if prep_q is None:
                for hc in range(2):
                    pf.append(load_gu_hc("wg", hc, f"wg{g}_{hc}"))
                    pf.append(load_gu_hc("wu", hc, f"wu{g}_{hc}"))

            def filler(n):
                for _ in range(n):
                    if prep_q is not None:
                        if prep_q:
                            prep_q.pop(0)()
                    elif fstate[0] < HCN:
                        ffn_unit(g, fstate[0], pf)
                        fstate[0] += 1

            load_xe(g, f"ldxe{g}")

            for eh in range(2):
                for tc in proj_eh(g, eh, wq_s[eh], p, "qproj"):
                    norm_apply(g, "q", eh, tc)
                filler(1 + eh)
            kq_transpose(g, QF, bQF, "qT", range(TCN))
            filler(1)
            has_ln = g + 2 < n_iters
            if has_ln:
                ln_stats(g + 2)

            if g + 1 < n_iters:
                wk_s = load_wa("wk", 0, f"wk0_{g}"), load_wa("wk", 1, f"wk1_{g}")
                for eh in range(2):
                    for tc in proj_eh(g + 1, eh, wk_s[eh], pn, "kproj"):
                        norm_apply(g + 1, "k", eh, tc)
                    filler(1 + eh)
                kq_transpose(g + 1, KF[pn], bKF[pn], "kT", range(TCN))
                filler(1)
                wv_s = load_wa("wv", 0, f"wv0_{g}"), load_wa("wv", 1, f"wv1_{g}")
                for eh in range(2):
                    for tc in proj_eh(g + 1, eh, wv_s[eh], pn, "vproj"):
                        P.op("act", (lambda e, eh=eh, tc=tc, pn=pn: e.copy(
                            VP[pn][:, :].rearrange("p (kc h c) -> p kc h c", kc=4, c=65)[:, tc, 8 * eh:8 * eh + 8, 0:64],
                            psA[:, 512 * tc:512 * tc + 512].rearrange(
                                "p (h c) -> p h c", h=8))),
                            reads=[bpsA[tc]], writes=[bVP[pn]],
                            tag=f"vEv{g}_{eh}_{tc}")
                    filler(1)
            else:
                filler(6)

            if dbg and g == 0:
                for nm, src_ap, src_b in [("d_xn", XN[0], bXN[0][3]),
                                          ("d_kf", KF[1], bKF[1]),
                                          ("d_vp", VP[1], bVP[1]),
                                          ("d_qf", QF, bQF)]:
                    P.op("gp", (lambda e, sem, nm=nm, src_ap=src_ap: e.dma_start(
                        dbg_outs[nm], src_ap[:, :]).then_inc(sem, 16)),
                        reads=[src_b], tag=f"dbg_{nm}")

            nwq0 = load_wa("wq", 0, f"wq0_{g + 1}") if g + 1 < n_iters else None

            wo_s = load_wa("wo", 0, f"wo0_{g}"), load_wa("wo", 1, f"wo1_{g}")
            for qb in range(TCN):
                attn_qb(g, qb, wo_s, filler)

            wa_i[0] += 1
            nwq1 = load_wa("wq", 1, f"wq1_{g + 1}") if g + 1 < n_iters else None

            if dbg and g == 0:
                P.op("gp", (lambda e, sem: e.dma_start(
                    dbg_outs["d_pt"], PT[:, :]).then_inc(sem, 16)),
                    reads=[bPT[0][0], bPT[0][1], bPT[0][2], bPT[0][3],
                           bPT[1][0], bPT[1][1], bPT[1][2], bPT[1][3]],
                    tag="dbg_pt")
                P.op("gp", (lambda e, sem: e.dma_start(
                    dbg_outs["d_sb"], SBUF_S[:, :]).then_inc(sem, 16)),
                    reads=[bSB4[0], bSB4[1], bSB4[2], bSB4[3]], tag="dbg_sb")

            if has_ln:
                ln_apply(g + 2)
            if prep_q is not None:
                while prep_q:
                    prep_q.pop(0)()
                for hc in range(2):
                    pf.append(load_gu_hc("wg", hc, f"wg{g}_{hc}"))
                    pf.append(load_gu_hc("wu", hc, f"wu{g}_{hc}"))
                while fstate[0] < HCN:
                    ffn_unit(g, fstate[0], pf)
                    fstate[0] += 1
            else:
                filler(2)
                while fstate[0] < HCN:
                    filler(1)
            if has_ln:
                xn_T(g + 2, range(TCN))

            if dbg and g == 0:
                P.op("gp", (lambda e, sem: e.dma_start(
                    dbg_outs["d_hid"], HID[:, :]).then_inc(sem, 16)),
                    reads=bHID12, tag="dbg_hid")

            for hc2 in range(12):
                down_group(g, hc2)
                if hc2 == 4 and g + 3 < n_iters:
                    load_x(g + 3, f"ldx{g + 3}")
            epilogue_tc(g, 0)
            epilogue_tc(g, 1)
            epilogue_tc(g, 2)
            epilogue_tc(g, 3)
            return (nwq0, nwq1)

        P.op("sp", lambda e, sem: e.dma_start(
            EB[:, :].rearrange("p (c n) -> p c n", c=2),
            eb_in.rearrange("c p n -> p c n")).then_inc(sem, 16),
            writes=[bEB], tag="ld_eb")
        P.op("sp", lambda e, sem: e.dma_start(IDT[:, :], id_in).then_inc(sem, 16),
             writes=[bIDT], tag="ld_id")

        def cst_fn(e):
            e.memset(CST[:, 0:1], EPS_LN)
            e.memset(CST[:, 1:2], C127)
            e.memset(CST[:, 2:3], C127SQ)
            return e.memset(CST[:, 3:4], -0.5)
        P.op("dve", cst_fn, writes=[bCST], tag="cst")
        for i in range(2):
            P.op("dve", (lambda e, i=i: e.memset(
                VP[i][:, :].rearrange("p (kc h c) -> p kc h c", kc=4, c=65)[:, :, :, 64:65],
                1.0)), writes=[bVP[i]], tag=f"ones{i}")

        load_x(0, "ldx0")
        ln_stats(0)
        ln_apply(0)
        xn_T(0, range(TCN))
        load_x(1, "ldx1")
        ln_stats(1)
        ln_apply(1)
        xn_T(1, range(TCN))
        load_x(2, "ldx2")
        prep_attn("wq")
        prep_attn("wk")
        prep_attn("wv")
        prep_attn("wo")
        wk_s = load_wa("wk", 0, "wk0_p"), load_wa("wk", 1, "wk1_p")
        for eh in range(2):
            for tc in proj_eh(0, eh, wk_s[eh], 0, "kproj"):
                norm_apply(0, "k", eh, tc)
        kq_transpose(0, KF[0], bKF[0], "kT", range(TCN))
        wv_s = load_wa("wv", 0, "wv0_p"), load_wa("wv", 1, "wv1_p")
        for eh in range(2):
            for tc in proj_eh(0, eh, wv_s[eh], 0, "vproj"):
                P.op("act", (lambda e, eh=eh, tc=tc: e.copy(
                    VP[0][:, :].rearrange("p (kc h c) -> p kc h c", kc=4, c=65)[:, tc, 8 * eh:8 * eh + 8, 0:64],
                    psA[:, 512 * tc:512 * tc + 512].rearrange(
                        "p (h c) -> p h c", h=8))),
                    reads=[bpsA[tc]], writes=[bVP[0]], tag=f"vEvp_{eh}_{tc}")
        prep_wd()
        pin0_pool = [(WD[0][:, 0:1024], bWD[0]), (WD[1][:, 0:1024], bWD[1])] + [
            (HID[:, 1024 * i:1024 * i + 1024], bHID12[i]) for i in range(6)]
        prep_q = []
        for q8 in range(8):
            for nm in ("wg", "wu"):
                prep_q.append(lambda nm=nm, q8=q8: prep_gu_chunk(
                    nm, q8, pin0_pool, ps=psB, bps=bpsB))

        wq_s = load_wa("wq", 0, "wq0_0"), load_wa("wq", 1, "wq1_0")
        for g in range(n_iters):
            wq_s = tile(g, wq_s, prep_q=prep_q if g == 0 else None)

        counts = P.finalize()

        def run(eng_obj, name):
            for (waits, fn, evt, tag, nd) in P.ops[name]:
                for w in waits:
                    eng_obj.wait_ge(sems[w.sem], w.val)
                if nd > 0:
                    fn(eng_obj, sems[evt.sem])
                else:
                    inst = fn(eng_obj)
                    if evt.used:
                        inst.then_inc(sems[evt.sem], 1)
            if name == "gp":
                for i in range(NDMASEM):
                    s_ = f"gp{i}"
                    if counts.get(s_, 0):
                        eng_obj.wait_ge(sems[s_], counts[s_])

        with nc.Block() as block:
            @block.tensor
            def _(t):
                run(t, "pe")

            @block.scalar
            def _(a):
                run(a, "act")

            @block.vector
            def _(v):
                run(v, "dve")

            @block.sync
            def _(s):
                run(s, "sp")

            @block.gpsimd
            def _(gp):
                run(gp, "gp")

    return nc


_EB_NP, _ID_NP = None, None


def kernel(**inputs):
    global _EB_NP, _ID_NP
    if _EB_NP is None:
        _EB_NP, _ID_NP = _make_consts()

    x = np.ascontiguousarray(np.asarray(inputs["L0_attn_x"], dtype=np.float32))
    ws = {
        "wq": np.ascontiguousarray(np.asarray(inputs["L0_attn_Wq"], dtype=np.float32)),
        "wk": np.ascontiguousarray(np.asarray(inputs["L0_attn_Wk"], dtype=np.float32)),
        "wv": np.ascontiguousarray(np.asarray(inputs["L0_attn_Wv"], dtype=np.float32)),
        "wo": np.ascontiguousarray(np.asarray(inputs["L0_attn_Wo"], dtype=np.float32)),
        "wg": np.ascontiguousarray(np.asarray(inputs["L0_ffn_w_gate"], dtype=np.int32)),
        "wu": np.ascontiguousarray(np.asarray(inputs["L0_ffn_w_up"], dtype=np.int32)),
        "wd": np.ascontiguousarray(np.asarray(inputs["L0_ffn_w_down"], dtype=np.int32)),
    }
    nc = build_nc(NTILES, dbg=False)
    in_maps = []
    for c in range(NCORES):
        m = {"x": x[SEQ_PER_CORE * c:SEQ_PER_CORE * (c + 1)],
             "eb": _EB_NP, "ident": _ID_NP}
        m.update(ws)
        in_maps.append(m)
    res = run_bass_kernel_spmd(nc, in_maps, core_ids=list(range(NCORES)))
    return np.concatenate([res.results[c]["out"] for c in range(NCORES)], axis=0)


# version 24 (final)
# speedup vs baseline: 1.0604x; 1.0313x over previous
import numpy as np
import ml_dtypes
import concourse.bass as bass
import concourse.mybir as mybir
from concourse.bass_utils import run_bass_kernel_spmd

F32 = mybir.dt.float32
BF16 = mybir.dt.bfloat16
I32 = mybir.dt.int32
AF = mybir.ActivationFunctionType
ALU = mybir.AluOpType
AX = mybir.AxisListType

B, T, D, H, HD, WIN, FF = 32, 2048, 1024, 16, 64, 128, 3072
NCORES = 8
SEQ_PER_CORE = B // NCORES
NT = 512
TILES_PER_SEQ = T // NT
NTILES = SEQ_PER_CORE * TILES_PER_SEQ
TCN = NT // 128
DCN = D // 128
HCN = FF // 128
EPS_LN = 1e-5
EPS_L2 = 1e-6
C127 = float(np.float32(1.0 / 127.0))
C127SQ = float(np.float32(1.0 / 16129.0))


class _Evt:
    __slots__ = ("sem", "ord", "used", "val")

    def __init__(self, sem, ordn):
        self.sem, self.ord, self.used, self.val = sem, ordn, False, None


class _Buf:
    __slots__ = ("name", "ready", "users")

    def __init__(self, name):
        self.name, self.ready, self.users = name, None, []


_ENG_SEM = {"pe": "pe", "act": "act", "dve": "dve", "sp": "sp", "gp": "gp"}
_DMA_ENGS = ("sp", "gp")
NDMASEM = 6


class _Plan:
    def __init__(self):
        self.ops = {e: [] for e in _ENG_SEM}
        self.nsem = {}
        self.marks = {e: {} for e in _ENG_SEM}
        self.dmaidx = {e: 0 for e in _DMA_ENGS}
        self.last_on_sem = {}

    def op(self, eng, fn, reads=(), writes=(), tag="", ndma=1):
        waits = {}

        def need(evt):
            if evt is None:
                return
            m = self.marks[eng].get(evt.sem)
            if m is not None and m >= evt.ord:
                return
            cur = waits.get(evt.sem)
            if cur is None or evt.ord > cur.ord:
                waits[evt.sem] = evt

        for b in reads:
            need(b.ready)
        for b in writes:
            need(b.ready)
            for u in b.users:
                need(u)
        if eng in _DMA_ENGS and ndma > 0:
            slot = self.dmaidx[eng] % NDMASEM
            self.dmaidx[eng] += 1
            sem = f"{eng}{slot}"
            need(self.last_on_sem.get(sem))
        elif eng == "gp":
            sem = "gpc"
        else:
            sem = _ENG_SEM[eng]
        for s_, evt in waits.items():
            evt.used = True
            self.marks[eng][s_] = evt.ord
        ordn = self.nsem.get(sem, 0)
        self.nsem[sem] = ordn + 1
        evt = _Evt(sem, ordn)
        if eng in _DMA_ENGS and ndma > 0:
            evt.used = True
            self.last_on_sem[sem] = evt
        self.ops[eng].append((list(waits.values()), fn, evt, tag,
                              ndma if eng in _DMA_ENGS else 0))
        for b in writes:
            b.ready = evt
            b.users = []
        for b in reads:
            b.users.append(evt)
        return evt

    def finalize(self):
        cnt = {}
        for e in _ENG_SEM:
            for (_w, _f, evt, _t, nd) in self.ops[e]:
                if evt.used:
                    inc = 16 * nd if nd > 0 else 1
                    c = cnt.get(evt.sem, 0) + inc
                    cnt[evt.sem] = c
                    evt.val = c
        return cnt


def _make_consts():
    slopes = (2.0 ** (-(8.0 * (np.arange(H) / 15.0)))).astype(np.float64)
    k = np.arange(128)[:, None]
    q = np.arange(128)[None, :]
    eb = np.zeros((2, 128, H * 128), np.float64)
    rel0 = k - q
    rel1 = 128 + k - q
    for h in range(H):
        e0 = np.where(rel0 >= 0, np.exp(-np.abs(rel0) * slopes[h]), 0.0)
        e1 = np.where(rel1 <= 127, np.exp(-np.abs(rel1) * slopes[h]), 0.0)
        eb[0][:, 128 * h:128 * h + 128] = e0
        eb[1][:, 128 * h:128 * h + 128] = e1
    return eb.astype(ml_dtypes.bfloat16), np.eye(128, dtype=ml_dtypes.bfloat16)


def build_nc(n_iters=NTILES, dbg=False):
    nc = bass.Bass("TRN2", target_bir_lowering=False)

    x_in = nc.dram_tensor("x", [SEQ_PER_CORE, T, D], F32, kind="ExternalInput").ap()
    w_in = {}
    for nm, shp, dt in [("wq", [D, D], F32), ("wk", [D, D], F32),
                        ("wv", [D, D], F32), ("wo", [D, D], F32),
                        ("wg", [FF, D], I32), ("wu", [FF, D], I32),
                        ("wd", [D, FF], I32)]:
        w_in[nm] = nc.dram_tensor(nm, shp, dt, kind="ExternalInput").ap()
    eb_in = nc.dram_tensor("eb", [2, 128, H * 128], BF16, kind="ExternalInput").ap()
    id_in = nc.dram_tensor("ident", [128, 128], BF16, kind="ExternalInput").ap()
    out = nc.dram_tensor("out", [SEQ_PER_CORE, T, D], F32, kind="ExternalOutput").ap()

    wT = {nm: nc.dram_tensor(nm + "T_s", [2, 128, 4096], BF16, kind="Internal").ap()
          for nm in ("wq", "wk", "wv", "wo")}
    wT["wg"] = nc.dram_tensor("wgT_s", [24, 128, 1024], BF16, kind="Internal").ap()
    wT["wu"] = nc.dram_tensor("wuT_s", [24, 128, 1024], BF16, kind="Internal").ap()
    wT["wd"] = nc.dram_tensor("wdT_s", [24, 128, 1024], BF16, kind="Internal").ap()

    dbg_outs = {}
    if dbg:
        for nm, shp, dt in [
            ("d_xn", [128, 4096], BF16), ("d_kf", [128, 4096], BF16),
            ("d_vp", [128, 4160], BF16), ("d_qf", [128, 4096], BF16),
            ("d_pt", [128, 4096], BF16), ("d_sb", [128, 4096], BF16),
            ("d_hid", [128, 12288], BF16),
        ]:
            dbg_outs[nm] = nc.dram_tensor(nm, shp, dt, kind="ExternalOutput").ap()

    P = _Plan()
    import contextlib
    stack = contextlib.ExitStack()
    sb = lambda name, shape, dt: stack.enter_context(nc.sbuf_tensor(name, shape, dt))
    ps = lambda name, shape, dt: stack.enter_context(nc.psum_tensor(name, shape, dt))
    mksem = lambda name: stack.enter_context(nc.semaphore(name))

    with stack:
        XL = sb("XL", [128, 4096], F32)
        XE = sb("XE", [128, 4096], F32)
        XN = [sb(f"XN{i}", [128, 4096], BF16) for i in range(2)]
        TM = sb("TM", [128, 4096], BF16)
        QF = sb("QF", [128, 4096], BF16)
        KF = [sb(f"KF{i}", [128, 4096], BF16) for i in range(2)]
        VP = [sb(f"VP{i}", [128, 4160], BF16) for i in range(2)]
        PT = sb("PT", [128, 4096], BF16)
        SBUF_S = sb("SBS", [128, 4096], BF16)
        HID = sb("HID", [128, 12288], BF16)
        GBUF = sb("GBUF", [128, 1024], F32)
        AOF = sb("AOF", [128, 1024], BF16)
        SQ = sb("SQ", [128, 1024], F32)
        TSC = sb("TSC", [128, 1024], BF16)
        EB = sb("EB", [128, 4096], BF16)
        IDT = sb("IDT", [128, 128], BF16)
        CST = sb("CST", [128, 4], F32)
        STAT = sb("STAT", [128, 256], F32)
        WA = [sb(f"WA{i}", [128, 4096], BF16) for i in range(3)]
        WGU = [sb(f"WGU{i}", [128, 3072], BF16) for i in range(3)]
        WD = [sb(f"WD{i}", [128, 2048], BF16) for i in range(2)]

        psA = ps("psA", [128, 2048], F32)
        psB = ps("psB", [128, 2048], F32)

        sems = {n: mksem(n) for n in ("pe", "act", "dve", "gpc")}
        for e in _DMA_ENGS:
            for i in range(NDMASEM):
                sems[f"{e}{i}"] = mksem(f"{e}{i}")

        bXL, bQF, bAOF = (_Buf(n) for n in ("XL", "QF", "AOF"))
        bXE4 = [_Buf(f"XE{t}") for t in range(TCN)]
        bTM4 = [_Buf(f"TM{i}") for i in range(TCN)]
        bXN = [[_Buf(f"XN{i}_{t}") for t in range(TCN)] for i in range(2)]
        bKF = [_Buf(f"KF{i}") for i in range(2)]
        bVP = [_Buf(f"VP{i}") for i in range(2)]
        bPT = [[_Buf(f"PT{c}_{b}") for b in range(4)] for c in range(2)]
        bSB4 = [_Buf(f"SB{q}") for q in range(TCN)]
        bHID12 = [_Buf(f"HID{i}") for i in range(12)]
        bG = _Buf("G")
        bSQ2 = [_Buf(f"SQ{i}") for i in range(2)]
        bTSC = [_Buf(f"TSC{i}") for i in range(2)]
        bEB, bIDT, bCST = _Buf("EB"), _Buf("IDT"), _Buf("CST")
        bNRM = [_Buf(f"NRM{i}") for i in range(8)]
        bJNK = _Buf("JNK")
        bQF4 = [_Buf(f"QFs{i}") for i in range(4)]
        bPT2 = [_Buf(f"PTs{i}") for i in range(2)]
        bRZ = _Buf("RZ")
        bpsA = [_Buf(f"psA{b}") for b in range(4)]
        bpsB = [_Buf(f"psB{b}") for b in range(4)]
        bWT = {n: _Buf("wt_" + n) for n in ("wq", "wk", "wv", "wo", "wg", "wu", "wd")}
        bWA = [_Buf(f"WA{i}") for i in range(3)]
        bWGU = [_Buf(f"WGU{i}") for i in range(3)]
        bWD = [_Buf(f"WD{i}") for i in range(2)]

        LNS4, LNSS4, LNM4, LNMSQ4, LNV4, LNR4, LNMR4 = (4 * i for i in range(7))
        RZ0 = 160
        st = lambda c, w=1: STAT[:, c:c + w]
        bLN1 = _Buf("LN1")

        wa_i, gu_i, wd_i = [0], [0], [0]

        def load_wa(nm, eh, tag):
            s = wa_i[0] % 3
            wa_i[0] += 1
            ap, bf = WA[s], bWA[s]
            src_ap = wT[nm][eh]
            P.op("sp", (lambda e, sem, ap=ap, src_ap=src_ap:
                        e.dma_start(ap[:, :], src_ap).then_inc(sem, 16)),
                 reads=[bWT[nm]], writes=[bf], tag=tag)
            return ap, bf

        gu9_i = [0]
        bGU9 = [_Buf(f"GU9_{i}") for i in range(9)]

        def load_gu_hc(nm, hc, tag):
            s = gu9_i[0] % 9
            gu9_i[0] += 1
            ap, off = WGU[s // 3], 1024 * (s % 3)
            bf = bGU9[s]
            P.op("sp", (lambda e, sem, ap=ap, off=off, src=wT[nm][hc]:
                        e.dma_start(ap[:, off:off + 1024], src).then_inc(sem, 16)),
                 reads=[bWT[nm]], writes=[bf, bWGU[s // 3]], tag=tag)
            return (ap, off), bf

        wd2_i = [0]

        def load_wd2(hc2, tag):
            pool = ((WD[0], bWD[0]), (WD[1], bWD[1]),
                    (WA[2][:, 0:2048], bWA[2]), (WA[2][:, 2048:4096], bWA[2]))
            s = wd2_i[0] % 4
            wd2_i[0] += 1
            ap, bf = pool[s]
            P.op("sp", (lambda e, sem, ap=ap, hc2=hc2: e.dma_start(
                ap[:, :].rearrange("p (i q) -> p i q", i=2),
                wT["wd"][2 * hc2:2 * hc2 + 2].rearrange("i p q -> p i q")
                ).then_inc(sem, 16)),
                 reads=[bWT["wd"]], writes=[bf], tag=tag)
            return ap, bf

        pin_pool = [(WD[0][:, 0:1024], bWD[0]), (WD[1][:, 0:1024], bWD[1])] + [
            (QF[:, 1024 * i:1024 * i + 1024], bQF4[i]) for i in range(4)]
        pinw_pool = [(WGU[i][:, 0:1536], bWGU[i]) for i in range(3)] + [
            (PT[:, 2048 * i:2048 * i + 1536], bPT2[i]) for i in range(2)]

        def _prep_in(nm, rows0, ncols, pool, idx, cols0=0):
            s = idx[0] % len(pool)
            q = "gp"
            idx[0] += 1
            ap, bf = pool[s]
            P.op(q, (lambda e, sem, ap=ap, nm=nm, rows0=rows0, ncols=ncols,
                     cols0=cols0:
                     e.dma_start(ap[:, 0:ncols],
                                 w_in[nm][rows0:rows0 + 128,
                                          cols0:cols0 + ncols]
                                 ).then_inc(sem, 16)),
                 writes=[bf], tag=f"pin_{nm}_{rows0}_{cols0}")
            return ap, bf

        def _prep_T(inap, inbf, nblk, bank, tag, ps=None, bps=None):
            ps = psA if ps is None else ps
            bps = bpsA if bps is None else bps
            def fn(e, inap=inap, nblk=nblk, bank=bank, ps=ps):
                last = None
                bc = ps[:, 512 * bank:512 * bank + 512].bitcast(BF16)
                if nblk > 8:
                    bc = ps[:, 512 * bank:512 * bank + 1024].bitcast(BF16)
                for i in range(nblk):
                    last = e.transpose(bc[:, 128 * i:128 * i + 128],
                                       inap[:, 128 * i:128 * i + 128], IDT[:, :])
                return last
            wrts = [bps[bank]] + ([bps[bank + 1]] if nblk > 8 else [])
            P.op("pe", fn, reads=[inbf, bIDT], writes=wrts, tag=tag)
            return wrts

        def prep_attn(nm):
            for eh in range(2):
                s = wa_i[0] % 3
                wa_i[0] += 1
                stg, bstg = WA[s], bWA[s]
                for r in range(4):
                    inap, inbf = _prep_in(nm, 512 * eh + 128 * r, 1024,
                                          pin_pool, wd_i)
                    pb = _prep_T(inap, inbf, 8, r % 4, f"pT_{nm}{eh}_{r}")
                    P.op("act", (lambda e, stg=stg, r=r, bank=r % 4: e.copy(
                        stg[:, :].rearrange("p (dc q) -> p dc q", q=512)[:, :, 128 * r:128 * r + 128],
                        psA[:, 512 * bank:512 * bank + 512].bitcast(BF16).rearrange(
                            "p (dc q) -> p dc q", dc=8))),
                        reads=pb, writes=[bstg], tag=f"pEv_{nm}{eh}_{r}")
                P.op("sp", (lambda e, sem, stg=stg, nm=nm, eh=eh:
                            e.dma_start(wT[nm][eh], stg[:, :]).then_inc(sem, 16)),
                     reads=[bstg], writes=[bWT[nm]], tag=f"pS_{nm}{eh}")

        def prep_gu_chunk(nm, q8, pool, ps=None, bps=None):
            pps = psA if ps is None else ps
            s = gu_i[0] % 3
            gu_i[0] += 1
            stg, bstg = WGU[s], bWGU[s]
            for j in range(3):
                inap, inbf = _prep_in(nm, 384 * q8 + 128 * j, 1024,
                                      pool, wd_i)
                pb = _prep_T(inap, inbf, 8, j % 4, f"pT_{nm}{q8}_{j}",
                             ps=ps, bps=bps)
                P.op("act", (lambda e, stg=stg, j=j, bank=j % 4, pps=pps: e.copy(
                    stg[:, :].rearrange("p (dc q) -> p dc q", q=384)[:, :, 128 * j:128 * j + 128],
                    pps[:, 512 * bank:512 * bank + 512].bitcast(BF16).rearrange(
                        "p (dc q) -> p dc q", dc=8))),
                    reads=pb, writes=[bstg], tag=f"pEv_{nm}{q8}_{j}")
            for j in range(3):
                P.op("sp", (lambda e, sem, stg=stg, nm=nm, q8=q8, j=j:
                            e.dma_start(
                    wT[nm][3 * q8 + j].rearrange("p (dc e) -> p dc e", dc=8),
                    stg[:, :].rearrange("p (dc j e) -> p j dc e",
                                        j=3, e=128)[:, j]).then_inc(sem, 16)),
                     reads=[bstg], writes=[bWT[nm]], tag=f"pS_{nm}{q8}_{j}")

        def prep_wd():
            for half in range(2):
                for r in range(8):
                    inap, inbf = _prep_in("wd", 128 * r, 1536, pinw_pool, gu_i,
                                          cols0=1536 * half)
                    pb = _prep_T(inap, inbf, 12, 2 * (r % 2), f"pT_wd{half}_{r}")
                    P.op("act", (lambda e, r=r, bank=2 * (r % 2): e.copy(
                        HID[:, :].rearrange("p (hc q) -> p hc q", q=1024)[:, 0:12, 128 * r:128 * r + 128],
                        psA[:, 512 * bank:512 * bank + 1024].bitcast(BF16)[:, 0:1536].rearrange(
                            "p (hc q) -> p hc q", hc=12))),
                        reads=pb, writes=bHID12, tag=f"pEv_wd{half}_{r}")
                for i in range(12):
                    P.op("sp", (lambda e, sem, i=i, half=half:
                                e.dma_start(wT["wd"][12 * half + i],
                                            HID[:, 1024 * i:1024 * i + 1024]
                                            ).then_inc(sem, 16)),
                         reads=bHID12, writes=[bWT["wd"]], tag=f"pS_wd{half}_{i}")

        def load_x(g, tag):
            s1, j1 = divmod(g, TILES_PER_SEQ)
            src = x_in[s1, NT * j1:NT * (j1 + 1), :].rearrange(
                "(tc tp) d -> tp tc d", tp=128)
            P.op("gp", (lambda e, sem, src=src: e.dma_start(
                XL[:, :].rearrange("p (tc d) -> p tc d", tc=TCN),
                src).then_inc(sem, 16)), writes=[bXL], tag=tag)

        def load_xe(g, tag):
            s1, j1 = divmod(g, TILES_PER_SEQ)
            src = x_in[s1, NT * j1:NT * (j1 + 1), :].rearrange(
                "(tc tp) d -> tp tc d", tp=128)
            P.op("gp", (lambda e, sem, src=src: e.dma_start(
                XE[:, :].rearrange("p (tc d) -> p tc d", tc=TCN),
                src).then_inc(sem, 16)), writes=bXE4, tag=tag)

        def ln_stats(gt):
            for tc in range(TCN):
                xsl = XL[:, 1024 * tc:1024 * (tc + 1)]
                P.op("act", (lambda e, xsl=xsl, tc=tc: e.activation(
                    GBUF[:, :], xsl, AF.Copy,
                    accum_out=st(LNS4 + tc))),
                    reads=[bXL], writes=[bG, bLN1], tag=f"lnsum{gt}_{tc}")
                P.op("act", (lambda e, xsl=xsl, tc=tc: e.activation(
                    GBUF[:, :], xsl, AF.Square,
                    accum_out=st(LNSS4 + tc))),
                    reads=[bXL], writes=[bG, bLN1], tag=f"lnss{gt}_{tc}")
            P.op("gp", lambda e: e.tensor_scalar_mul(
                st(LNM4, 4), st(LNS4, 4), -1.0 / D),
                reads=[bLN1], writes=[bLN1], ndma=0)
            P.op("gp", lambda e: e.tensor_mul(
                st(LNMSQ4, 4), st(LNM4, 4), st(LNM4, 4)),
                reads=[bLN1], writes=[bLN1], ndma=0)
            P.op("gp", lambda e: e.tensor_scalar_mul(
                st(LNV4, 4), st(LNSS4, 4), 1.0 / D),
                reads=[bLN1], writes=[bLN1], ndma=0)
            P.op("gp", lambda e: e.tensor_sub(
                st(LNV4, 4), st(LNV4, 4), st(LNMSQ4, 4)),
                reads=[bLN1], writes=[bLN1], ndma=0)
            P.op("gp", lambda e: e.tensor_scalar_add(
                st(LNV4, 4), st(LNV4, 4), EPS_LN),
                reads=[bLN1], writes=[bLN1], ndma=0)
            P.op("act", lambda e: e.activation(
                st(LNR4, 4), st(LNV4, 4), AF.Sqrt),
                reads=[bLN1], writes=[bLN1])
            P.op("dve", lambda e: e.reciprocal(st(LNR4, 4), st(LNR4, 4)),
                 reads=[bLN1], writes=[bLN1])
            P.op("gp", lambda e: e.tensor_mul(
                st(LNMR4, 4), st(LNM4, 4), st(LNR4, 4)),
                reads=[bLN1], writes=[bLN1], ndma=0)

        def ln_apply(gt):
            for tc in range(TCN):
                P.op("act", (lambda e, tc=tc: e.activation(
                    TM[:, 1024 * tc:1024 * (tc + 1)],
                    XL[:, 1024 * tc:1024 * (tc + 1)], AF.Copy,
                    scale=st(LNR4 + tc))),
                    reads=[bXL, bLN1], writes=[bTM4[tc]], tag=f"lnap{gt}_{tc}")
                P.op("dve", (lambda e, tc=tc: e.tensor_tensor(
                    TM[:, 1024 * tc:1024 * (tc + 1)],
                    TM[:, 1024 * tc:1024 * (tc + 1)],
                    st(LNMR4 + tc).broadcast_to([128, 1024]), ALU.add)),
                    reads=[bLN1, bTM4[tc]], writes=[bTM4[tc]],
                    tag=f"lnad{gt}_{tc}")

        def xn_T(gt, tcs):
            p1 = gt % 2
            for tc in tcs:
                def tfn(e, tc=tc):
                    last = None
                    bc = psA[:, 512 * tc:512 * tc + 512].bitcast(BF16)
                    for dc in range(DCN):
                        last = e.transpose(
                            bc[:, 128 * dc:128 * dc + 128],
                            TM[:, 1024 * tc + 128 * dc:1024 * tc + 128 * dc + 128],
                            IDT[:, :])
                    return last
                P.op("pe", tfn, reads=[bTM4[tc], bIDT], writes=[bpsA[tc]],
                     tag=f"xnT{gt}_{tc}")
                P.op("act", (lambda e, tc=tc, p1=p1: e.copy(
                    XN[p1][:, :].rearrange("p (dc t) -> p dc t", dc=DCN)[:, :, 128 * tc:128 * tc + 128],
                    psA[:, 512 * tc:512 * tc + 512].bitcast(BF16).rearrange(
                        "p (dc t) -> p dc t", dc=DCN))),
                    reads=[bpsA[tc]], writes=[bXN[p1][tc]], tag=f"xnEv{gt}_{tc}")

        def norm_apply(g1, which, eh, tc):
            r = 2 * tc + eh
            SS8, N8 = 32 + 16 * r, 40 + 16 * r
            bN = bNRM[r]
            sqo = 512 * (tc % 2)
            bS = bSQ2[tc % 2]
            P.op("act", (lambda e, tc=tc, sqo=sqo: e.activation(
                SQ[:, sqo:sqo + 512], psA[:, 512 * tc:512 * tc + 512], AF.Square)),
                reads=[bpsA[tc]], writes=[bS], tag=f"{which}sq{g1}_{eh}_{tc}")
            P.op("dve", (lambda e, SS8=SS8, sqo=sqo: e.tensor_reduce(
                st(SS8, 8).rearrange("p (h u) -> p h u", u=1),
                SQ[:, sqo:sqo + 512].rearrange("p (h c) -> p h c", h=8),
                AX.X, ALU.add)), reads=[bS], writes=[bN])
            P.op("act", (lambda e, SS8=SS8, N8=N8: e.activation(
                st(N8, 8), st(SS8, 8), AF.Sqrt)), reads=[bN], writes=[bN])
            P.op("dve", (lambda e, N8=N8: e.tensor_scalar_max(
                st(N8, 8), st(N8, 8), EPS_L2)), reads=[bN], writes=[bN])
            P.op("dve", (lambda e, N8=N8: e.reciprocal(st(N8, 8), st(N8, 8))),
                 reads=[bN], writes=[bN])
            P.op("dve", (lambda e, eh=eh, tc=tc, N8=N8: e.tensor_tensor(
                TM[:, 1024 * tc + 512 * eh:1024 * tc + 512 * eh + 512].rearrange(
                    "p (h c) -> p h c", h=8),
                psA[:, 512 * tc:512 * tc + 512].rearrange("p (h c) -> p h c", h=8),
                st(N8, 8).unsqueeze(2).broadcast_to([128, 8, 64]), ALU.mult)),
                reads=[bpsA[tc], bN], writes=[bTM4[tc]], tag=f"{which}nrm{g1}_{eh}_{tc}")

        def kq_transpose(g1, DST, bDST, tag, tcs):
            for tc in tcs:
                def tfn(e, tc=tc):
                    last = None
                    bc = psA[:, 512 * tc:512 * tc + 512].bitcast(BF16)
                    for ec in range(DCN):
                        last = e.transpose(
                            bc[:, 128 * ec:128 * ec + 128],
                            TM[:, 1024 * tc + 128 * ec:1024 * tc + 128 * ec + 128],
                            IDT[:, :])
                    return last
                P.op("pe", tfn, reads=[bTM4[tc], bIDT], writes=[bpsA[tc]],
                     tag=f"{tag}{g1}_{tc}")
                P.op("act", (lambda e, tc=tc, DST=DST: e.copy(
                    DST[:, :].rearrange("p (ec t) -> p ec t", ec=DCN)[:, :, 128 * tc:128 * tc + 128],
                    psA[:, 512 * tc:512 * tc + 512].bitcast(BF16).rearrange(
                        "p (ec t) -> p ec t", ec=DCN))),
                    reads=[bpsA[tc]], writes=[bDST], tag=f"{tag}Ev{g1}_{tc}")

        def proj_eh(g1, eh, wslot, p1, tag):
            wap, wab = wslot
            for tc in range(TCN):
                def fn(e, eh=eh, tc=tc, wap=wap, p1=p1):
                    last = None
                    for dc in range(DCN):
                        last = e.matmul(
                            psA[:, 512 * tc:512 * tc + 512],
                            XN[p1][:, 512 * dc + 128 * tc:512 * dc + 128 * tc + 128],
                            wap[:, 512 * dc:512 * dc + 512],
                            start=(dc == 0), stop=(dc == DCN - 1))
                    return last
                P.op("pe", fn, reads=[bXN[p1][tc], wab], writes=[bpsA[tc]],
                     tag=f"{tag}{g1}_{eh}_{tc}")
                yield tc

        def ffn_unit(g, hc, pf):
            p = g % 2
            (gap, goff), gab = pf.pop(0)
            (uap, uoff), uab = pf.pop(0)
            if hc + 2 < HCN:
                pf.append(load_gu_hc("wg", hc + 2, f"wg{g}_{hc + 2}"))
                pf.append(load_gu_hc("wu", hc + 2, f"wu{g}_{hc + 2}"))
            go = 512 * (hc % 2)
            uo = 1024 + 512 * (hc % 2)

            def gfn(e, gap=gap, goff=goff, p=p, go=go):
                last = None
                for dc in range(DCN):
                    last = e.matmul(
                        psB[:, go:go + 512],
                        gap[:, goff + 128 * dc:goff + 128 * dc + 128],
                        XN[p][:, 512 * dc:512 * dc + 512],
                        start=(dc == 0), stop=(dc == DCN - 1))
                return last
            P.op("pe", gfn, reads=bXN[p] + [gab], writes=[bpsB[hc % 2]],
                 tag=f"gate{g}_{hc}")
            P.op("act", (lambda e, hc=hc, go=go: e.activation(
                TSC[:, 512 * (hc % 2):512 * (hc % 2) + 512], psB[:, go:go + 512],
                AF.Silu, scale=CST[:, 1:2])),
                reads=[bpsB[hc % 2], bCST], writes=[bTSC[hc % 2]],
                tag=f"silu{g}_{hc}")

            def ufn(e, uap=uap, uoff=uoff, p=p, uo=uo):
                last = None
                for dc in range(DCN):
                    last = e.matmul(
                        psB[:, uo:uo + 512],
                        uap[:, uoff + 128 * dc:uoff + 128 * dc + 128],
                        XN[p][:, 512 * dc:512 * dc + 512],
                        start=(dc == 0), stop=(dc == DCN - 1))
                return last
            P.op("pe", ufn, reads=bXN[p] + [uab], writes=[bpsB[2 + hc % 2]],
                 tag=f"up{g}_{hc}")
            P.op("dve", (lambda e, hc=hc, uo=uo: e.tensor_tensor(
                HID[:, 512 * hc:512 * hc + 512], psB[:, uo:uo + 512],
                TSC[:, 512 * (hc % 2):512 * (hc % 2) + 512], ALU.mult)),
                reads=[bpsB[2 + hc % 2], bTSC[hc % 2]], writes=[bHID12[hc // 2]],
                tag=f"hid{g}_{hc}")

        _DFB = {0: (psA, 0), 1: (psA, 1024), 2: (psB, 0), 3: (psB, 1024)}

        def down_group(g, hc2, slot):
            wdap, wdb_ = slot

            def dfn(e, hc2=hc2, wdap=wdap):
                last = None
                for i in range(2):
                    hc = 2 * hc2 + i
                    for tc in range(TCN):
                        pt, off = _DFB[tc]
                        for dh in range(2):
                            last = e.matmul(
                                pt[:, off + 512 * dh:off + 512 * dh + 512],
                                HID[:, 512 * hc + 128 * tc:512 * hc + 128 * tc + 128],
                                wdap[:, 1024 * i + 512 * dh:1024 * i + 512 * dh + 512],
                                start=(hc == 0), stop=(hc == HCN - 1))
                return last
            wr = [bpsA[0], bpsA[1], bpsA[2], bpsA[3],
                  bpsB[0], bpsB[1], bpsB[2], bpsB[3]]
            P.op("pe", dfn, reads=[bHID12[hc2], wdb_], writes=wr,
                 tag=f"down{g}_{hc2}")

        def epilogue_tc(g, tc):
            s1, j1 = divmod(g, TILES_PER_SEQ)
            pt, off = _DFB[tc]
            bp = bpsA if tc < 2 else bpsB
            P.op("act", (lambda e, pt=pt, off=off: e.activation(
                GBUF[:, :], pt[:, off:off + 1024], AF.Silu, scale=CST[:, 2:3])),
                reads=[bp[2 * (tc % 2)], bp[2 * (tc % 2) + 1], bCST],
                writes=[bG], tag=f"gsilu{g}_{tc}")
            P.op("dve", (lambda e, tc=tc: e.tensor_tensor(
                GBUF[:, :], GBUF[:, :],
                SBUF_S[:, 1024 * tc:1024 * tc + 1024], ALU.mult)),
                reads=[bG, bSB4[tc]], writes=[bG], tag=f"sg{g}_{tc}")
            P.op("dve", (lambda e, tc=tc: e.tensor_tensor(
                XE[:, 1024 * tc:1024 * tc + 1024],
                GBUF[:, :],
                XE[:, 1024 * tc:1024 * tc + 1024], ALU.add)),
                reads=[bG, bXE4[tc]], writes=[bXE4[tc]], tag=f"addx{g}_{tc}")
            P.op("gp", (lambda e, sem, s1=s1, j1=j1, tc=tc: e.dma_start(
                out[s1, NT * j1 + 128 * tc:NT * j1 + 128 * tc + 128, :],
                XE[:, 1024 * tc:1024 * tc + 1024]).then_inc(sem, 16)),
                reads=[bXE4[tc]], tag=f"st{g}_{tc}")

        def attn_qb(g, qb, wo_s, filler):
            p, pn = g % 2, (g + 1) % 2
            j = g % TILES_PER_SEQ
            have_c1 = (qb < 3) or (j < 3 and g + 1 < n_iters)
            for ci in range(2 if have_c1 else 1):
                kc = qb + ci
                kpar, kcc = (p, kc) if kc < TCN else (pn, 0)
                for par in range(2):
                    def skfn(e, par=par, kpar=kpar, kcc=kcc, qb=qb):
                        last = None
                        r0 = 64 * par
                        for h in range(par, 16, 2):
                            ecb = 512 * (h // 2)
                            last = e.matmul(
                                psA[:, 128 * h:128 * h + 128],
                                KF[kpar][r0:r0 + 64, ecb + 128 * kcc:ecb + 128 * kcc + 128],
                                QF[r0:r0 + 64, ecb + 128 * qb:ecb + 128 * qb + 128],
                                start=True, stop=True)
                        return last
                    P.op("pe", skfn, reads=[bKF[kpar], bQF],
                         writes=[bpsA[0], bpsA[1], bpsA[2], bpsA[3]],
                         tag=f"qk{g}_{qb}_{ci}_{par}")
                P.op("act", lambda e: e.activation(
                    st(200), CST[:, 0:1], AF.Exp),
                    reads=[bCST], writes=[bJNK], tag=f"dxp{g}_{qb}_{ci}")
                for b in range(4):
                    P.op("act", (lambda e, b=b, ci=ci: e.activation(
                        PT[:, 2048 * ci + 512 * b:2048 * ci + 512 * b + 512],
                        psA[:, 512 * b:512 * b + 512], AF.Exp)),
                        reads=[bpsA[b]], writes=[bPT[ci][b]],
                        tag=f"exp{g}_{qb}_{ci}_{b}")
                    P.op("dve", (lambda e, b=b, ci=ci: e.tensor_tensor(
                        PT[:, 2048 * ci + 512 * b:2048 * ci + 512 * b + 512],
                        PT[:, 2048 * ci + 512 * b:2048 * ci + 512 * b + 512],
                        EB[:, 2048 * ci + 512 * b:2048 * ci + 512 * b + 512],
                        ALU.mult)),
                        reads=[bEB, bPT[ci][b]], writes=[bPT[ci][b]],
                        tag=f"emul{g}_{qb}_{ci}_{b}")
                filler(1)

            for b in range(4):
                def avfn(e, b=b, qb=qb, have_c1=have_c1, p=p, g=g):
                    last = None
                    for h in range(4 * b, 4 * b + 4):
                        for ci in range(2 if have_c1 else 1):
                            kc = qb + ci
                            vpar, vcc = (p, kc) if kc < TCN else ((g + 1) % 2, 0)
                            last = e.matmul(
                                psA[:, 128 * h:128 * h + 65],
                                PT[:, 2048 * ci + 128 * h:2048 * ci + 128 * h + 128],
                                VP[vpar][:, 1040 * vcc + 65 * h:1040 * vcc + 65 * h + 65],
                                start=(ci == 0),
                                stop=(ci == (1 if have_c1 else 0)))
                    return last
                rd = [bPT[0][b], bVP[p]]
                if have_c1:
                    rd += [bPT[1][b], bVP[(g + 1) % 2] if qb == 3 else bVP[p]]
                P.op("pe", avfn, reads=rd, writes=[bpsA[b]],
                     tag=f"av{g}_{qb}_{b}")

            P.op("dve", lambda e: e.reciprocal(
                st(RZ0, 16).rearrange("p (h u) -> p h u", u=1),
                psA[:, :].rearrange("p (h c) -> p h c", h=16)[:, :, 64:65]),
                reads=[bpsA[0], bpsA[1], bpsA[2], bpsA[3]], writes=[bRZ],
                tag=f"rz{g}_{qb}")
            P.op("dve", (lambda e, qb=qb: e.tensor_tensor(
                TM[:, 1024 * qb:1024 * qb + 1024].rearrange("p (h c) -> p h c", h=16),
                psA[:, :].rearrange("p (h c) -> p h c", h=16)[:, :, 0:64],
                st(RZ0, 16).unsqueeze(2).broadcast_to([128, 16, 64]), ALU.mult)),
                reads=[bpsA[0], bpsA[1], bpsA[2], bpsA[3], bRZ],
                writes=[bTM4[qb]], tag=f"aonrm{g}_{qb}")
            filler(1)

            def aotfn(e, qb=qb):
                last = None
                bc = psA[:, 0:512].bitcast(BF16)
                for ec in range(DCN):
                    last = e.transpose(
                        bc[:, 128 * ec:128 * ec + 128],
                        TM[:, 1024 * qb + 128 * ec:1024 * qb + 128 * ec + 128],
                        IDT[:, :])
                return last
            P.op("pe", aotfn, reads=[bTM4[qb], bIDT], writes=[bpsA[0]],
                 tag=f"aoT{g}_{qb}")
            P.op("act", lambda e: e.copy(AOF[:, :], psA[:, 0:512].bitcast(BF16)),
                 reads=[bpsA[0]], writes=[bAOF], tag=f"aoEv{g}_{qb}")

            for half in range(2):
                wap, wab = wo_s[half]

                def wofn(e, wap=wap, half=half):
                    last = None
                    for ec in range(DCN):
                        last = e.matmul(
                            psA[:, 1024 + 512 * half:1024 + 512 * half + 512],
                            AOF[:, 128 * ec:128 * ec + 128],
                            wap[:, 512 * ec:512 * ec + 512],
                            start=(ec == 0), stop=(ec == DCN - 1))
                    return last
                P.op("pe", wofn, reads=[bAOF, wab], writes=[bpsA[2 + half]],
                     tag=f"wo{g}_{qb}_{half}")
                P.op("act", (lambda e, qb=qb, half=half: e.copy(
                    SBUF_S[:, 1024 * qb + 512 * half:1024 * qb + 512 * half + 512],
                    psA[:, 1024 + 512 * half:1024 + 512 * half + 512])),
                    reads=[bpsA[2 + half]], writes=[bSB4[qb]],
                    tag=f"sEv{g}_{qb}_{half}")

        def tile(g, wq_s, prep_q=None):
            p, pn = g % 2, (g + 1) % 2
            fstate = [0]
            pf = []
            if prep_q is None:
                for hc in range(2):
                    pf.append(load_gu_hc("wg", hc, f"wg{g}_{hc}"))
                    pf.append(load_gu_hc("wu", hc, f"wu{g}_{hc}"))

            def filler(n):
                for _ in range(n):
                    if prep_q is not None:
                        if prep_q:
                            prep_q.pop(0)()
                    elif fstate[0] < HCN:
                        ffn_unit(g, fstate[0], pf)
                        fstate[0] += 1

            load_xe(g, f"ldxe{g}")

            for eh in range(2):
                for tc in proj_eh(g, eh, wq_s[eh], p, "qproj"):
                    norm_apply(g, "q", eh, tc)
                filler(1 + eh)
            kq_transpose(g, QF, bQF, "qT", range(TCN))
            filler(1)
            has_ln = g + 2 < n_iters
            if has_ln:
                ln_stats(g + 2)

            if g + 1 < n_iters:
                wk_s = load_wa("wk", 0, f"wk0_{g}"), load_wa("wk", 1, f"wk1_{g}")
                for eh in range(2):
                    for tc in proj_eh(g + 1, eh, wk_s[eh], pn, "kproj"):
                        norm_apply(g + 1, "k", eh, tc)
                    filler(1 + eh)
                kq_transpose(g + 1, KF[pn], bKF[pn], "kT", range(TCN))
                filler(1)
                wv_s = load_wa("wv", 0, f"wv0_{g}"), load_wa("wv", 1, f"wv1_{g}")
                for eh in range(2):
                    for tc in proj_eh(g + 1, eh, wv_s[eh], pn, "vproj"):
                        P.op("act", (lambda e, eh=eh, tc=tc, pn=pn: e.copy(
                            VP[pn][:, :].rearrange("p (kc h c) -> p kc h c", kc=4, c=65)[:, tc, 8 * eh:8 * eh + 8, 0:64],
                            psA[:, 512 * tc:512 * tc + 512].rearrange(
                                "p (h c) -> p h c", h=8))),
                            reads=[bpsA[tc]], writes=[bVP[pn]],
                            tag=f"vEv{g}_{eh}_{tc}")
                    filler(1)
            else:
                filler(6)

            if dbg and g == 0:
                for nm, src_ap, src_b in [("d_xn", XN[0], bXN[0][3]),
                                          ("d_kf", KF[1], bKF[1]),
                                          ("d_vp", VP[1], bVP[1]),
                                          ("d_qf", QF, bQF)]:
                    P.op("gp", (lambda e, sem, nm=nm, src_ap=src_ap: e.dma_start(
                        dbg_outs[nm], src_ap[:, :]).then_inc(sem, 16)),
                        reads=[src_b], tag=f"dbg_{nm}")

            nwq0 = load_wa("wq", 0, f"wq0_{g + 1}") if g + 1 < n_iters else None

            wo_s = load_wa("wo", 0, f"wo0_{g}"), load_wa("wo", 1, f"wo1_{g}")
            for qb in range(TCN):
                attn_qb(g, qb, wo_s, filler)

            wa_i[0] += 1
            nwq1 = load_wa("wq", 1, f"wq1_{g + 1}") if g + 1 < n_iters else None

            if dbg and g == 0:
                P.op("gp", (lambda e, sem: e.dma_start(
                    dbg_outs["d_pt"], PT[:, :]).then_inc(sem, 16)),
                    reads=[bPT[0][0], bPT[0][1], bPT[0][2], bPT[0][3],
                           bPT[1][0], bPT[1][1], bPT[1][2], bPT[1][3]],
                    tag="dbg_pt")
                P.op("gp", (lambda e, sem: e.dma_start(
                    dbg_outs["d_sb"], SBUF_S[:, :]).then_inc(sem, 16)),
                    reads=[bSB4[0], bSB4[1], bSB4[2], bSB4[3]], tag="dbg_sb")

            if has_ln:
                ln_apply(g + 2)
            if prep_q is not None:
                while prep_q:
                    prep_q.pop(0)()
                for hc in range(2):
                    pf.append(load_gu_hc("wg", hc, f"wg{g}_{hc}"))
                    pf.append(load_gu_hc("wu", hc, f"wu{g}_{hc}"))
                while fstate[0] < HCN:
                    ffn_unit(g, fstate[0], pf)
                    fstate[0] += 1
            else:
                filler(2)
                while fstate[0] < HCN:
                    filler(1)
            if has_ln:
                xn_T(g + 2, range(TCN))

            if dbg and g == 0:
                P.op("gp", (lambda e, sem: e.dma_start(
                    dbg_outs["d_hid"], HID[:, :]).then_inc(sem, 16)),
                    reads=bHID12, tag="dbg_hid")

            wd_slots = [load_wd2(0, f"wd{g}_0"), load_wd2(1, f"wd{g}_1")]
            for hc2 in range(12):
                if hc2 + 2 < 12:
                    wd_slots.append(load_wd2(hc2 + 2, f"wd{g}_{hc2 + 2}"))
                down_group(g, hc2, wd_slots[hc2])
                if hc2 == 4 and g + 3 < n_iters:
                    load_x(g + 3, f"ldx{g + 3}")
            epilogue_tc(g, 0)
            epilogue_tc(g, 1)
            epilogue_tc(g, 2)
            epilogue_tc(g, 3)
            return (nwq0, nwq1)

        P.op("sp", lambda e, sem: e.dma_start(
            EB[:, :].rearrange("p (c n) -> p c n", c=2),
            eb_in.rearrange("c p n -> p c n")).then_inc(sem, 16),
            writes=[bEB], tag="ld_eb")
        P.op("sp", lambda e, sem: e.dma_start(IDT[:, :], id_in).then_inc(sem, 16),
             writes=[bIDT], tag="ld_id")

        def cst_fn(e):
            e.memset(CST[:, 0:1], EPS_LN)
            e.memset(CST[:, 1:2], C127)
            e.memset(CST[:, 2:3], C127SQ)
            return e.memset(CST[:, 3:4], -0.5)
        P.op("dve", cst_fn, writes=[bCST], tag="cst")
        for i in range(2):
            P.op("dve", (lambda e, i=i: e.memset(
                VP[i][:, :].rearrange("p (kc h c) -> p kc h c", kc=4, c=65)[:, :, :, 64:65],
                1.0)), writes=[bVP[i]], tag=f"ones{i}")

        load_x(0, "ldx0")
        ln_stats(0)
        ln_apply(0)
        xn_T(0, range(TCN))
        load_x(1, "ldx1")
        ln_stats(1)
        ln_apply(1)
        xn_T(1, range(TCN))
        load_x(2, "ldx2")
        prep_attn("wq")
        prep_attn("wk")
        prep_attn("wv")
        prep_attn("wo")
        wk_s = load_wa("wk", 0, "wk0_p"), load_wa("wk", 1, "wk1_p")
        for eh in range(2):
            for tc in proj_eh(0, eh, wk_s[eh], 0, "kproj"):
                norm_apply(0, "k", eh, tc)
        kq_transpose(0, KF[0], bKF[0], "kT", range(TCN))
        wv_s = load_wa("wv", 0, "wv0_p"), load_wa("wv", 1, "wv1_p")
        for eh in range(2):
            for tc in proj_eh(0, eh, wv_s[eh], 0, "vproj"):
                P.op("act", (lambda e, eh=eh, tc=tc: e.copy(
                    VP[0][:, :].rearrange("p (kc h c) -> p kc h c", kc=4, c=65)[:, tc, 8 * eh:8 * eh + 8, 0:64],
                    psA[:, 512 * tc:512 * tc + 512].rearrange(
                        "p (h c) -> p h c", h=8))),
                    reads=[bpsA[tc]], writes=[bVP[0]], tag=f"vEvp_{eh}_{tc}")
        prep_wd()
        pin0_pool = [(WD[0][:, 0:1024], bWD[0]), (WD[1][:, 0:1024], bWD[1])] + [
            (HID[:, 1024 * i:1024 * i + 1024], bHID12[i]) for i in range(6)]
        prep_q = []
        for q8 in range(8):
            for nm in ("wg", "wu"):
                prep_q.append(lambda nm=nm, q8=q8: prep_gu_chunk(
                    nm, q8, pin0_pool, ps=psB, bps=bpsB))

        wq_s = load_wa("wq", 0, "wq0_0"), load_wa("wq", 1, "wq1_0")
        for g in range(n_iters):
            wq_s = tile(g, wq_s, prep_q=prep_q if g == 0 else None)

        counts = P.finalize()

        def run(eng_obj, name):
            for (waits, fn, evt, tag, nd) in P.ops[name]:
                for w in waits:
                    eng_obj.wait_ge(sems[w.sem], w.val)
                if nd > 0:
                    fn(eng_obj, sems[evt.sem])
                else:
                    inst = fn(eng_obj)
                    if evt.used:
                        inst.then_inc(sems[evt.sem], 1)
            if name == "gp":
                for i in range(NDMASEM):
                    s_ = f"gp{i}"
                    if counts.get(s_, 0):
                        eng_obj.wait_ge(sems[s_], counts[s_])

        with nc.Block() as block:
            @block.tensor
            def _(t):
                run(t, "pe")

            @block.scalar
            def _(a):
                run(a, "act")

            @block.vector
            def _(v):
                run(v, "dve")

            @block.sync
            def _(s):
                run(s, "sp")

            @block.gpsimd
            def _(gp):
                run(gp, "gp")

    return nc


_EB_NP, _ID_NP = None, None


def kernel(**inputs):
    global _EB_NP, _ID_NP
    if _EB_NP is None:
        _EB_NP, _ID_NP = _make_consts()

    x = np.ascontiguousarray(np.asarray(inputs["L0_attn_x"], dtype=np.float32))
    ws = {
        "wq": np.ascontiguousarray(np.asarray(inputs["L0_attn_Wq"], dtype=np.float32)),
        "wk": np.ascontiguousarray(np.asarray(inputs["L0_attn_Wk"], dtype=np.float32)),
        "wv": np.ascontiguousarray(np.asarray(inputs["L0_attn_Wv"], dtype=np.float32)),
        "wo": np.ascontiguousarray(np.asarray(inputs["L0_attn_Wo"], dtype=np.float32)),
        "wg": np.ascontiguousarray(np.asarray(inputs["L0_ffn_w_gate"], dtype=np.int32)),
        "wu": np.ascontiguousarray(np.asarray(inputs["L0_ffn_w_up"], dtype=np.int32)),
        "wd": np.ascontiguousarray(np.asarray(inputs["L0_ffn_w_down"], dtype=np.int32)),
    }
    nc = build_nc(NTILES, dbg=False)
    in_maps = []
    for c in range(NCORES):
        m = {"x": x[SEQ_PER_CORE * c:SEQ_PER_CORE * (c + 1)],
             "eb": _EB_NP, "ident": _ID_NP}
        m.update(ws)
        in_maps.append(m)
    res = run_bass_kernel_spmd(nc, in_maps, core_ids=list(range(NCORES)))
    return np.concatenate([res.results[c]["out"] for c in range(NCORES)], axis=0)
